# version 06
# baseline (speedup 1.0000x reference)
import math
import numpy as np
from contextlib import ExitStack

import concourse.bass as bass
import concourse.mybir as mybir
from concourse.bass_utils import run_bass_kernel_spmd

F32 = mybir.dt.float32
BF16 = mybir.dt.bfloat16
I32 = mybir.dt.int32
AF = mybir.ActivationFunctionType
ALU = mybir.AluOpType
AX = mybir.AxisListType

S = 4096
D = 2048
TOK = 1024
NQC = 32
SCALE = 1.0 / math.sqrt(128.0)
EBIAS = -12.0
EPS = 1e-5


def build_nc(dbg=()):
    nc = bass.Bass("TRN2", target_bir_lowering=False)
    st = ExitStack()
    sems = {}
    cnt = {}

    def sem(name):
        if name not in sems:
            sems[name] = st.enter_context(nc.semaphore(name))
            cnt[name] = 0
        return sems[name]

    def inc(inst, name, n=1):
        inst.then_inc(sem(name), n)
        cnt[name] += n
        return inst

    def w(eng, name, val):
        eng.wait_ge(sem(name), val)

    def wch(eng):
        eng.wait_ge(sem("s_vch"), cnt["s_vch"])

    x_full = nc.dram_tensor("x_full", [S, D], F32, kind="ExternalInput").ap()
    x_own = nc.dram_tensor("x_own", [TOK, D], F32, kind="ExternalInput").ap()
    wqk = nc.dram_tensor("wqk", [D, 1024], F32, kind="ExternalInput").ap()
    wv = nc.dram_tensor("wv", [D, D], F32, kind="ExternalInput").ap()
    wo = nc.dram_tensor("wo", [D, D], F32, kind="ExternalInput").ap()
    bw = nc.dram_tensor("bw", [D, D], F32, kind="ExternalInput").ap()
    bb_in = nc.dram_tensor("bb_in", [1, D], F32, kind="ExternalInput").ap()
    rw_in = nc.dram_tensor("rw_in", [8, D], F32, kind="ExternalInput").ap()
    rb_in = nc.dram_tensor("rb_in", [1, 8], F32, kind="ExternalInput").ap()
    lA_in = nc.dram_tensor("lA_in", [8, D, 16], F32, kind="ExternalInput").ap()
    lB_in = nc.dram_tensor("lB_in", [128, D], F32, kind="ExternalInput").ap()
    hq_in = nc.dram_tensor("hq_in", [1, 1], I32, kind="ExternalInput").ap()

    y_out = nc.dram_tensor("y_out", [TOK, D], F32, kind="ExternalOutput").ap()

    x_bf = nc.dram_tensor("x_bf", [S, D], BF16, kind="Internal").ap()
    xo_bf = nc.dram_tensor("xo_bf", [TOK, D], BF16, kind="Internal").ap()
    bw_bf = nc.dram_tensor("bw_bf", [D, D], BF16, kind="Internal").ap()
    cin = nc.dram_tensor("cin", [4, S], F32, kind="Internal")
    cout = nc.dram_tensor("cout", [16, S], F32, kind="Internal")

    dbg_t = {}
    def dbgo(name, shape, dt=F32):
        if name in dbg:
            dbg_t[name] = nc.dram_tensor("dbg_" + name, shape, dt, kind="ExternalOutput").ap()
    dbgo("cout", [16, S])

    sb = st.enter_context
    ident_bf = sb(nc.sbuf_tensor("ident_bf", [128, 128], BF16))
    ident_f32 = sb(nc.sbuf_tensor("ident_f32", [128, 128], F32))
    ones_row = sb(nc.sbuf_tensor("ones_row", [1, 128], F32))
    ones_tok = sb(nc.sbuf_tensor("ones_tok", [1, 512], F32))
    ones_col_bf = sb(nc.sbuf_tensor("ones_col_bf", [128, 1], BF16))
    e8_bf = sb(nc.sbuf_tensor("e8_bf", [8, 128], BF16))
    bias_exp = sb(nc.sbuf_tensor("bias_exp", [128, 1], F32))
    rwT = sb(nc.sbuf_tensor("rwT", [128, 16, 8], F32))
    hqsb = sb(nc.sbuf_tensor("hqsb", [1, 1], I32))
    vT = sb(nc.sbuf_tensor("vT", [128, 16, TOK], BF16))

    rhq = st.enter_context(nc.gpsimd.register("rhq"))
    ro1 = st.enter_context(nc.gpsimd.register("ro1"))

    phA = ExitStack()
    sbA = phA.enter_context
    qk_sb = sbA(nc.sbuf_tensor("qk_sb", [128, 4, 2, S], BF16))
    xT_sb = sbA(nc.sbuf_tensor("xT_sb", [128, 16, 1024], BF16))
    wqk_sb = sbA(nc.sbuf_tensor("wqk_sb", [128, 2, 16, 128], BF16))
    ekeep = [sbA(nc.sbuf_tensor(f"ekeep{i}", [128, S], BF16)) for i in range(2)]
    Cp = [sbA(nc.sbuf_tensor(f"Cp{i}", [128, S], BF16)) for i in range(2)]
    dpart = [sbA(nc.sbuf_tensor(f"dpart{i}", [128, 4], F32)) for i in range(2)]
    dt1 = sbA(nc.sbuf_tensor("dt1", [128, 1], F32))
    dt2 = sbA(nc.sbuf_tensor("dt2", [128, 1], F32))
    dsum = sbA(nc.sbuf_tensor("dsum", [128, 1], F32))
    rrec = sbA(nc.sbuf_tensor("rrec", [128, 1], F32))
    wvsb = sbA(nc.sbuf_tensor("wvsb", [128, 16, 128], BF16))
    gtmp = sbA(nc.sbuf_tensor("gtmp", [128, S - 3072], BF16))
    w_stage = [sbA(nc.sbuf_tensor(f"wst{i}", [1, 512], F32)) for i in range(2)]
    w2st = [sbA(nc.sbuf_tensor(f"w2st{i}", [1, TOK], F32)) for i in range(2)]

    def g1a(qt, hl, f, sl):
        return ((qt * 4 + hl) * 2 + f) * 2 + sl

    psA_ctx = nc.psum_tensor("sA", [128, 1024], F32)
    psB_ctx = nc.psum_tensor("sB", [128, 1024], F32)
    psw0_ctx = nc.psum_tensor("psw0", [1, 512], F32)
    psw1_ctx = nc.psum_tensor("psw1", [1, 512], F32)
    pq0_ctx = nc.psum_tensor("pq0", [128, 512], F32)
    pq1_ctx = nc.psum_tensor("pq1", [128, 512], F32)
    sA = phA.enter_context(psA_ctx)
    sB = phA.enter_context(psB_ctx)
    psw = [phA.enter_context(psw0_ctx), phA.enter_context(psw1_ctx)]
    pq = [phA.enter_context(pq0_ctx), phA.enter_context(pq1_ctx)]
    sT = [sA, sB]
    pw = [sA, sB]

    def emit_proj_group(t, gi, qt, hl, f, sl):
        k2 = (qt * 4 + hl) * 2 + f
        if sl == 0:
            if hl == 0 and f == 0:
                w(t, "s_xt", 256 * (qt + 1))
            w(t, "s_wk", 16 * (k2 + 1))
        if gi >= 2:
            w(t, "s_evq", gi - 1)
        for dc in range(16):
            mi = t.matmul(pq[gi % 2][:, :], wqk_sb[:, k2 % 2, dc, :],
                          xT_sb[:, dc, sl * 512:(sl + 1) * 512],
                          start=(dc == 0), stop=(dc == 15))
        inc(mi, "s_mmq", 1)

    def emit_v_group(t, gv):
        fc, sl = divmod(gv, 2)
        if gv == 0:
            w(t, "s_xtv", 256)
            w(t, "s_evq", 64)
        w(t, "s_wv", 16 * (fc + 1))
        if gv >= 2:
            w(t, "s_evv", gv - 1)
        for dc in range(16):
            mi = t.matmul(pq[gv % 2][:, :], wvsb[:, dc, :],
                          xT_sb[:, dc, sl * 512:sl * 512 + 512],
                          start=(dc == 0), stop=(dc == 15))
        inc(mi, "s_mmv", 1)

    def emit_scores_chunk(t, hs, j):
        J = hs * NQC + j
        for i in range(4):
            Pi = J * 4 + i
            if Pi >= 2:
                w(t, "s_exp", Pi - 1)
            ps = sT[Pi % 2]
            lhs = qk_sb[:, hs, 0, j * 128:(j + 1) * 128]
            for half in range(2):
                k0 = i * 1024 + half * 512
                mi = t.matmul(ps[:, half * 512:(half + 1) * 512],
                              lhs, qk_sb[:, hs, 1, k0:k0 + 512],
                              start=True, stop=True)
            inc(mi, "s_sc", 1)

    def emit_fold(t, hs):
        w(t, "s_mac", NQC * (hs + 1))
        w(t, "s_macg", NQC * (hs + 1))
        for seg in range(8):
            Sg = hs * 8 + seg
            if Sg >= 2:
                w(t, "s_wev", Sg - 1)
            t.matmul(psw[Sg % 2][:, :], ones_col_bf[:, :],
                     Cp[0][:, seg * 512:(seg + 1) * 512],
                     start=True, stop=False)
            mi = t.matmul(psw[Sg % 2][:, :], ones_col_bf[:, :],
                          Cp[1][:, seg * 512:(seg + 1) * 512],
                          start=False, stop=True)
            inc(mi, "s_fold", 1)

    MSPL = 3072

    def emit_mac_chunk(v, hs, j):
        J = hs * NQC + j
        w(v, "s_exp", 4 * (J + 1))
        dp = dpart[J % 2]
        inc(v.tensor_add(dt1[:, :], dp[:, 0:1], dp[:, 1:2]), "s_vch", 1)
        inc(v.tensor_add(dt2[:, :], dp[:, 2:3], dp[:, 3:4]), "s_vch", 1)
        wch(v)
        inc(v.tensor_add(dsum[:, :], dt1[:, :], dt2[:, :]), "s_vch", 1)
        wch(v)
        if J >= 1:
            w(v, "s_macg", J)
        inc(v.reciprocal(rrec[:, :], dsum[:, :]), "s_rr", 1)
        w(v, "s_rr", J + 1)
        if j < 2:
            if hs > 0:
                w(v, "s_fold", 8 * hs)
            mac = v.tensor_scalar(Cp[j % 2][:, 0:MSPL], ekeep[J % 2][:, 0:MSPL],
                                  rrec[:, :], None, ALU.mult)
        else:
            mac = v.scalar_tensor_tensor(Cp[j % 2][:, 0:MSPL], ekeep[J % 2][:, 0:MSPL],
                                         rrec[:, :], Cp[j % 2][:, 0:MSPL],
                                         ALU.mult, ALU.add)
        inc(mac, "s_mac", 1)

    def emit_macg(g, hs, j):
        J = hs * NQC + j
        w(g, "s_exp", 4 * (J + 1))
        w(g, "s_rr", J + 1)
        if J >= 1:
            w(g, "s_macg", J)
        rb = rrec[:, :].broadcast_to([128, S - MSPL])
        if j < 2:
            if hs > 0:
                w(g, "s_fold", 8 * hs)
            mac = g.tensor_tensor(Cp[j % 2][:, MSPL:S], ekeep[J % 2][:, MSPL:S],
                                  rb, ALU.mult)
        else:
            inc(g.tensor_tensor(gtmp[:, :], ekeep[J % 2][:, MSPL:S], rb, ALU.mult),
                "s_gch", 1)
            w(g, "s_gch", cnt["s_gch"])
            mac = g.tensor_tensor(Cp[j % 2][:, MSPL:S], Cp[j % 2][:, MSPL:S],
                                  gtmp[:, :], ALU.add)
        inc(mac, "s_macg", 1)

    def emit_wev(v, hs):
        for seg in range(8):
            Sg = hs * 8 + seg
            w(v, "s_fold", Sg + 1)
            if Sg >= 2:
                w(v, "s_cind", 16 * (Sg - 1))
            inc(v.tensor_copy(w_stage[Sg % 2][:, :], psw[Sg % 2][:, :]),
                "s_wev", 1)

    def emit_evac(v, gi, qt, hl, f, sl):
        w(v, "s_mmq", gi + 1)
        t0 = qt * 1024 + sl * 512
        inc(v.tensor_copy(qk_sb[:, hl, f, t0:t0 + 512], pq[gi % 2][:, :]), "s_evq", 1)

    def emit_evac_v(v, gv):
        fc, sl = divmod(gv, 2)
        w(v, "s_mmv", gv + 1)
        inc(v.tensor_copy(vT[:, fc, sl * 512:(sl + 1) * 512], pq[gv % 2][:, :]),
            "s_evv", 1)

    def weave_v(hs, j, nxtv):
        return hs >= 1 and j % 3 == 0 and nxtv < 32

    with nc.Block("A") as blk:
        @blk.gpsimd
        def _(g):
            g.memset(ident_bf[:, :], 0.0)
            g.affine_select(out=ident_bf[:, :], in_=ident_bf[:, :],
                            compare_op=ALU.not_equal, fill=1.0, base=0,
                            pattern=[[-1, 128]], channel_multiplier=1)
            g.memset(ident_f32[:, :], 0.0)
            g.affine_select(out=ident_f32[:, :], in_=ident_f32[:, :],
                            compare_op=ALU.not_equal, fill=1.0, base=0,
                            pattern=[[-1, 128]], channel_multiplier=1)
            g.memset(ones_row[:, :], 1.0)
            g.memset(ones_tok[:, :], 1.0)
            g.memset(ones_col_bf[:, :], 1.0)
            g.memset(bias_exp[:, :], EBIAS)
            g.memset(e8_bf[:, :], 1.0)
            g.affine_select(out=e8_bf[:, :], in_=e8_bf[:, :], compare_op=ALU.is_ge,
                            fill=0.0, base=0, pattern=[[1, 128]], channel_multiplier=-16)
            inc(g.affine_select(out=e8_bf[:, :], in_=e8_bf[:, :], compare_op=ALU.is_ge,
                                fill=0.0, base=15, pattern=[[-1, 128]], channel_multiplier=16),
                "s_const", 1)
            inc(g.dma_start(hqsb[:, :], hq_in), "s_pn", 16)
            for i in range(4):
                inc(g.dma_start(x_bf[i * 1024:(i + 1) * 1024, :],
                                x_full[i * 1024:(i + 1) * 1024, :]), "s_xc", 16)
            inc(g.dma_start(xo_bf, x_own), "s_xoc", 16)
            for qt in range(4):
                for hl in range(4):
                    for f in range(2):
                        k2 = (qt * 4 + hl) * 2 + f
                        if k2 >= 2:
                            w(g, "s_mmq", 2 * k2 - 2)
                        c0 = f * 512 + hl * 128
                        src = bass.AP(wqk.tensor, c0,
                                      [[1024, 128], [128 * 1024, 16], [1, 128]])
                        inc(g.dma_start(wqk_sb[:, k2 % 2, :, :], src), "s_wk", 16)
            def wv_load(fc):
                if fc >= 1:
                    w(g, "s_mmv", 2 * fc)
                src = bass.AP(wv.tensor, fc * 128, [[2048, 128], [128 * 2048, 16], [1, 128]])
                inc(g.dma_start(wvsb[:, :, :], src), "s_wv", 16)
            wv_load(0)
            nxtvg = 0
            for hs in range(4):
                for j in range(NQC):
                    emit_macg(g, hs, j)
                    if weave_v(hs, j, nxtvg):
                        nxtvg += 1
                        if nxtvg % 2 == 0 and nxtvg // 2 < 16:
                            wv_load(nxtvg // 2)
            w(g, "s_cind", 512)
            inc(g.collective_compute(
                "AllGather", ALU.bypass,
                ins=[cin.ap().opt()], outs=[cout.ap().opt()],
                replica_groups=[[0, 1, 2, 3], [4, 5, 6, 7]]), "s_cc", 1)
            w(g, "s_cc", 1)
            w(g, "s_pn", 16)
            g.reg_load(rhq, hqsb[:1, :1])
            g.reg_mul(rhq, rhq, TOK)
            for h in range(16):
                if h >= 2:
                    w(g, "s_pwmm", h - 1)
                g.reg_mov(ro1, rhq)
                g.reg_add(ro1, ro1, h * S)
                inc(g.dma_start(w2st[h % 2][:, :],
                                bass.AP(cout.ap().tensor, ro1, [[S, 1], [1, TOK]])),
                    "s_w16", 16)
            if "cout" in dbg_t:
                inc(g.dma_start(dbg_t["cout"], cout.ap()), "s_dbg", 16)
                w(g, "s_dbg", 16)
            for i in range(2):
                inc(g.dma_start(bw_bf[i * 1024:(i + 1) * 1024, :],
                                bw[i * 1024:(i + 1) * 1024, :]), "s_bwc", 16)

        @blk.sync
        def _(sy):
            for qt in range(4):
                w(sy, "s_xc", 16 * (qt + 1))
                if qt > 0:
                    w(sy, "s_mmq", 16 * qt)
                for dc in range(16):
                    inc(sy.dma_start(xT_sb[:, dc, :],
                                     x_bf[qt * 1024:(qt + 1) * 1024, dc * 128:(dc + 1) * 128],
                                     transpose=True), "s_xt", 16)
            w(sy, "s_xoc", 16)
            w(sy, "s_mmq", 64)
            for dc in range(16):
                inc(sy.dma_start(xT_sb[:, dc, 0:TOK],
                                 xo_bf[:, dc * 128:(dc + 1) * 128],
                                 transpose=True), "s_xtv", 16)
            for Sg in range(32):
                hs, seg = divmod(Sg, 8)
                w(sy, "s_wev", Sg + 1)
                inc(sy.dma_start(cin.ap()[hs:hs + 1, seg * 512:(seg + 1) * 512],
                                 w_stage[Sg % 2][:, :]), "s_cind", 16)

        @blk.tensor
        def _(t):
            w(t, "s_const", 1)
            for qt in range(4):
                for hl in range(4):
                    for f in range(2):
                        for sl in range(2):
                            emit_proj_group(t, g1a(qt, hl, f, sl), qt, hl, f, sl)
            nxtv = 0
            for hs in range(4):
                w(t, "s_evq", g1a(3, hs, 1, 1) + 1)
                for j in range(NQC):
                    emit_scores_chunk(t, hs, j)
                    if weave_v(hs, j, nxtv):
                        emit_v_group(t, nxtv)
                        nxtv += 1
                emit_fold(t, hs)
            while nxtv < 32:
                emit_v_group(t, nxtv)
                nxtv += 1
            w(t, "s_exp", 512)
            for h in range(16):
                w(t, "s_w16", 16 * (h + 1))
                if h >= 2:
                    w(t, "s_evw", h - 1)
                for half in range(2):
                    mi = t.matmul(pw[h % 2][:, half * 512:(half + 1) * 512],
                                  ones_row[:, :],
                                  w2st[h % 2][:, half * 512:(half + 1) * 512],
                                  start=True, stop=True)
                inc(mi, "s_pwmm", 1)

        @blk.scalar
        def _(sc):
            w(sc, "s_const", 1)
            for hs in range(4):
                for j in range(NQC):
                    J = hs * NQC + j
                    for i in range(4):
                        Pi = J * 4 + i
                        w(sc, "s_sc", Pi + 1)
                        if i == 0 and J >= 2:
                            w(sc, "s_mac", J - 1)
                            w(sc, "s_macg", J - 1)
                        dst = ekeep[J % 2][:, i * 1024:(i + 1) * 1024]
                        inc(sc.activation(dst, sT[Pi % 2][:, :], AF.Exp,
                                          bias=bias_exp[:, :], scale=SCALE,
                                          accum_out=dpart[J % 2][:, i:i + 1]),
                            "s_exp", 1)

        @blk.vector
        def _(v):
            for qt in range(4):
                for hl in range(4):
                    for f in range(2):
                        for sl in range(2):
                            emit_evac(v, g1a(qt, hl, f, sl), qt, hl, f, sl)
            nxtv = 0
            for hs in range(4):
                for j in range(NQC):
                    emit_mac_chunk(v, hs, j)
                    if weave_v(hs, j, nxtv):
                        emit_evac_v(v, nxtv)
                        nxtv += 1
                emit_wev(v, hs)
            while nxtv < 32:
                emit_evac_v(v, nxtv)
                nxtv += 1
            for k in range(16):
                w(v, "s_pwmm", k + 1)
                w(v, "s_evv", 2 * k + 2)
                inc(v.tensor_mul(vT[:, k, :], vT[:, k, :], pw[k % 2][:, :]),
                    "s_evw", 1)

    phA.close()

    phB = ExitStack()
    sbB = phB.enter_context
    ysb = sbB(nc.sbuf_tensor("ysb", [128, 8, D], F32))
    x1fm = sbB(nc.sbuf_tensor("x1fm", [128, 16, TOK], BF16))
    xrf32 = sbB(nc.sbuf_tensor("xrf32", [128, TOK], F32))
    wslab = sbB(nc.sbuf_tensor("wslab", [128, 16, 512], BF16))
    logits_fm = sbB(nc.sbuf_tensor("logits_fm", [8, TOK], F32))
    gate_fm = sbB(nc.sbuf_tensor("gate_fm", [8, TOK], BF16))
    grep_sb = sbB(nc.sbuf_tensor("grep_sb", [128, TOK], BF16))
    midg = sbB(nc.sbuf_tensor("midg", [128, TOK], BF16))
    sqbuf = sbB(nc.sbuf_tensor("sqbuf", [128, TOK], F32))
    lssqA = sbB(nc.sbuf_tensor("lssqA", [128, 1], F32))
    ltm = sbB(nc.sbuf_tensor("ltm", [128, 8], F32))
    gm1 = sbB(nc.sbuf_tensor("gm1", [128, 1], F32))
    gnm1 = sbB(nc.sbuf_tensor("gnm1", [128, 1], F32))
    gge = sbB(nc.sbuf_tensor("gge", [128, 8], F32))
    gmsk = sbB(nc.sbuf_tensor("gmsk", [128, 8], F32))
    gm2 = sbB(nc.sbuf_tensor("gm2", [128, 1], F32))
    gz = sbB(nc.sbuf_tensor("gz", [128, 8], F32))
    gkeep = sbB(nc.sbuf_tensor("gkeep", [128, 8], F32))
    gzk = sbB(nc.sbuf_tensor("gzk", [128, 8], F32))
    gss = sbB(nc.sbuf_tensor("gss", [128, 1], F32))
    grs = sbB(nc.sbuf_tensor("grs", [128, 1], F32))
    gtm = sbB(nc.sbuf_tensor("gtm", [128, 8], F32))
    gbf = sbB(nc.sbuf_tensor("gbf", [128, 8], BF16))
    ls1 = sbB(nc.sbuf_tensor("ls1", [128, 1], F32))
    lssq = sbB(nc.sbuf_tensor("lssq", [128, 1], F32))
    lmean = sbB(nc.sbuf_tensor("lmean", [128, 1], F32))
    lmsq = sbB(nc.sbuf_tensor("lmsq", [128, 1], F32))
    lm2t = sbB(nc.sbuf_tensor("lm2t", [128, 1], F32))
    lvar = sbB(nc.sbuf_tensor("lvar", [128, 1], F32))
    lsd = sbB(nc.sbuf_tensor("lsd", [128, 1], F32))
    lrstd = sbB(nc.sbuf_tensor("lrstd", [128, 1], F32))
    bb_sb = sbB(nc.sbuf_tensor("bb_sb", [1, D], F32))
    rb_sb = sbB(nc.sbuf_tensor("rb_sb", [1, 8], F32))
    lacat = sbB(nc.sbuf_tensor("lacat", [128, 16, 128], BF16))
    lbcat = sbB(nc.sbuf_tensor("lbcat", [128, D], BF16))
    rw_sb = sbB(nc.sbuf_tensor("rw_sb", [8, 512], F32))

    PA = [phB.enter_context(nc.psum_tensor("PA0", [128, 512], F32)),
          phB.enter_context(nc.psum_tensor("PA1", [128, 512], F32))]
    PB = [phB.enter_context(nc.psum_tensor("PB0", [128, 128], F32)),
          phB.enter_context(nc.psum_tensor("PB1", [128, 128], F32))]
    PL = phB.enter_context(nc.psum_tensor("PL", [8, TOK], F32))
    PGB = phB.enter_context(nc.psum_tensor("PGB", [8, 128], BF16))
    pgt = PB[0][:, 0:8]

    def emit_ln_vector(v, tt, sa_name, sv_name, gate_name, gate_cnt):
        ap = ysb[:, tt, :]
        w(v, gate_name, gate_cnt)
        inc(v.reduce_sum(ls1[:, :], ap, axis=AX.X), "s_vch", 1)
        wch(v)
        inc(v.tensor_scalar(lmean[:, :], ls1[:, :], 1.0 / D, None, ALU.mult), "s_vch", 1)
        w(v, sa_name, 3 * tt + 2)
        inc(v.tensor_add(lssq[:, :], lssq[:, :], lssqA[:, :]), "s_vch", 1)
        wch(v)
        inc(v.tensor_scalar(lmsq[:, :], lssq[:, :], 1.0 / D, None, ALU.mult), "s_vch", 1)
        wch(v)
        inc(v.tensor_mul(lm2t[:, :], lmean[:, :], lmean[:, :]), "s_vch", 1)
        wch(v)
        inc(v.tensor_sub(lvar[:, :], lmsq[:, :], lm2t[:, :]), "s_vch", 1)
        wch(v)
        inc(v.tensor_scalar(lvar[:, :], lvar[:, :], EPS, None, ALU.add), sv_name, 1)
        w(v, sa_name, 3 * tt + 3)
        inc(v.reciprocal(lrstd[:, :], lsd[:, :]), "s_vch", 1)
        wch(v)
        inc(v.tensor_scalar(ap, ap, lmean[:, :], lrstd[:, :],
                            ALU.subtract, ALU.mult), sv_name, 1)

    def emit_ln_scalar(sc, tt, sa_name, sv_name, gate_name, gate_cnt):
        w(sc, gate_name, gate_cnt)
        if tt > 0:
            w(sc, sv_name, 2 * tt - 1)
        inc(sc.activation(sqbuf[:, :], ysb[:, tt, 0:TOK], AF.Square,
                          accum_out=lssqA[:, :]), sa_name, 1)
        inc(sc.activation(sqbuf[:, :], ysb[:, tt, TOK:D], AF.Square,
                          accum_out=lssq[:, :]), sa_name, 1)
        w(sc, sv_name, 2 * tt + 1)
        inc(sc.activation(lsd[:, :], lvar[:, :], AF.Sqrt), sa_name, 1)

    with nc.Block("B") as blk:
        @blk.gpsimd
        def _(g):
            inc(g.dma_start(bb_sb[:, :], bb_in), "s_ldb", 16)
            inc(g.dma_start(rb_sb[:, :], rb_in), "s_ldb", 16)
            inc(g.dma_start(lbcat[:, :], lB_in), "s_ldb", 16)
            inc(g.dma_start(rw_sb[:, :], rw_in[:, 0:512]), "s_rw", 16)
            for dc in range(16):
                src = bass.AP(lA_in.tensor, dc * 128 * 16,
                              [[16, 128], [D * 16, 8], [1, 16]])
                dst = lacat[:, dc, :].rearrange("p (e r) -> p e r", e=8)
                inc(g.dma_start(dst, src), "s_lda", 16)
            for qi in range(1, 4):
                w(g, "s_rwt", 4 * qi)
                inc(g.dma_start(rw_sb[:, :], rw_in[:, qi * 512:(qi + 1) * 512]),
                    "s_rw", 16)
            for fb in range(4):
                if fb >= 1:
                    w(g, "s_mmwo", 8 * fb)
                src = bass.AP(wo.tensor, fb * 512, [[2048, 128], [128 * 2048, 16], [1, 512]])
                inc(g.dma_start(wslab[:, :, :], src), "s_wo", 16)

        @blk.sync
        def _(sy):
            for tt in range(8):
                inc(sy.dma_start(ysb[:, tt, :], x_own[tt * 128:(tt + 1) * 128, :]),
                    "s_xow", 16)
            w(sy, "s_bwc", 32)
            w(sy, "s_mmwo", 32)
            for fb in range(4):
                if fb >= 1:
                    w(sy, "s_mmf", 8 * fb)
                for dc in range(16):
                    inc(sy.dma_start(wslab[:, dc, :],
                                     bw_bf[fb * 512:(fb + 1) * 512, dc * 128:(dc + 1) * 128],
                                     transpose=True), "s_wbt", 16)
            for tt in range(8):
                w(sy, "s_ln2v", 2 * (tt + 1))
                inc(sy.dma_start(y_out[tt * 128:(tt + 1) * 128, :], ysb[:, tt, :]),
                    "s_od", 16)
            w(sy, "s_od", 128)

        @blk.tensor
        def _(t):
            w(t, "s_rw", 16)
            for dc in range(16):
                if dc >= 1:
                    w(t, "s_rwev", dc)
                if dc % 4 == 0 and dc > 0:
                    w(t, "s_rw", 16 * (dc // 4 + 1))
                inc(t.transpose(PB[0][:, 0:8], rw_sb[:, (dc % 4) * 128:(dc % 4 + 1) * 128],
                                ident_f32[0:8, 0:8]), "s_rwt", 1)
            for fb in range(4):
                w(t, "s_wo", 16 * (fb + 1))
                if fb == 0:
                    w(t, "s_evw", 16)
                for tt in range(8):
                    gw = fb * 8 + tt
                    if gw >= 2:
                        w(t, "s_ya", gw - 1)
                    for cc in range(16):
                        mi = t.matmul(PA[gw % 2][:, :],
                                      vT[:, cc, tt * 128:(tt + 1) * 128],
                                      wslab[:, cc, :],
                                      start=(cc == 0), stop=(cc == 15))
                    inc(mi, "s_mmwo", 1)
            def router_mm(t, dc):
                w(t, "s_evt2", dc + 1)
                for half in range(2):
                    mi = t.matmul(PL[:, half * 512:(half + 1) * 512],
                                  rwT[:, dc, :], xrf32[:, half * 512:(half + 1) * 512],
                                  start=False, stop=(dc == 15))
                inc(mi, "s_mmr", 1)
            w(t, "s_ldb", 32)
            t.matmul(PL[:, 0:512], rb_sb[:, :], ones_tok[:, :], start=True, stop=False)
            t.matmul(PL[:, 512:1024], rb_sb[:, :], ones_tok[:, :], start=True, stop=False)
            for dc in range(16):
                if dc >= 1:
                    router_mm(t, dc - 1)
                for tt in range(8):
                    it = dc * 8 + tt
                    if dc == 0:
                        w(t, "s_ln1v", 2 * tt + 2)
                    if it >= 2:
                        w(t, "s_evt", it - 1)
                    inc(t.transpose(PB[it % 2][:, :], ysb[:, tt, dc * 128:(dc + 1) * 128],
                                    ident_f32[:, :]), "s_mmt", 1)
            router_mm(t, 15)
            for tt in range(8):
                w(t, "s_lfm", 1)
                if tt > 0:
                    w(t, "s_gltm", tt)
                    w(t, "s_gdv", 2 * tt)
                inc(t.transpose(pgt, logits_fm[0:8, tt * 128:(tt + 1) * 128],
                                ident_f32[0:8, 0:8]), "s_gpe", 1)
                w(t, "s_gdv", 2 * tt + 1)
                inc(t.transpose(PGB[:, :], gbf[:, :], ident_bf[:, :]), "s_gpe", 1)
            w(t, "s_gdv", 16)
            w(t, "s_lda", 256)
            for half in range(2):
                inc(t.matmul(PA[half][:, :], e8_bf[:, :],
                             gate_fm[:, half * 512:(half + 1) * 512],
                             start=True, stop=True), "s_mmg", 1)
            for half in range(2):
                w(t, "s_evg", half + 1)
                for dc in range(16):
                    mi = t.matmul(PA[half][:, :], lacat[:, dc, :],
                                  x1fm[:, dc, half * 512:(half + 1) * 512],
                                  start=(dc == 0), stop=(dc == 15))
                inc(mi, "s_mmm", 1)
            w(t, "s_evm", 2)
            w(t, "s_ldb", 48)
            for fb in range(4):
                w(t, "s_wbt", 256 * (fb + 1))
                for tt in range(8):
                    gf = fb * 8 + tt
                    if gf >= 2:
                        w(t, "s_yf", gf - 1)
                    t.matmul(PA[gf % 2][:, :], ones_row[:, :],
                             bb_sb[:, fb * 512:(fb + 1) * 512],
                             start=True, stop=False)
                    for dc in range(16):
                        t.matmul(PA[gf % 2][:, :],
                                 x1fm[:, dc, tt * 128:(tt + 1) * 128],
                                 wslab[:, dc, :],
                                 start=False, stop=False)
                    mi = t.matmul(PA[gf % 2][:, :],
                                  midg[:, tt * 128:(tt + 1) * 128],
                                  lbcat[:, fb * 512:(fb + 1) * 512],
                                  start=False, stop=True)
                    inc(mi, "s_mmf", 1)

        @blk.scalar
        def _(sc):
            for tt in range(8):
                emit_ln_scalar(sc, tt, "s_ln1a", "s_ln1v", "s_ya", 24 + tt + 1)
            for tt in range(8):
                w(sc, "s_gv2", tt + 1)
                inc(sc.activation(gz[:, :], ltm[:, :], AF.Exp, bias=gnm1[:, :],
                                  scale=1.0), "s_gact", 1)
            for tt in range(8):
                emit_ln_scalar(sc, tt, "s_ln2a", "s_ln2v", "s_yf", 24 + tt + 1)

        @blk.vector
        def _(v):
            for dc in range(16):
                w(v, "s_rwt", dc + 1)
                inc(v.tensor_copy(rwT[:, dc, :], PB[0][:, 0:8]), "s_rwev", 1)
            for fb in range(4):
                for tt in range(8):
                    gw = fb * 8 + tt
                    w(v, "s_mmwo", gw + 1)
                    if fb == 0:
                        w(v, "s_xow", 16 * (tt + 1))
                    inc(v.tensor_add(ysb[:, tt, fb * 512:(fb + 1) * 512],
                                     ysb[:, tt, fb * 512:(fb + 1) * 512],
                                     PA[gw % 2][:, :]), "s_ya", 1)
            for tt in range(8):
                emit_ln_vector(v, tt, "s_ln1a", "s_ln1v", "s_ya", 24 + tt + 1)
            for dc in range(16):
                if dc > 0:
                    w(v, "s_mmr", dc)
                for tt in range(8):
                    it = dc * 8 + tt
                    w(v, "s_mmt", it + 1)
                    inc(v.tensor_copy(x1fm[:, dc, tt * 128:(tt + 1) * 128],
                                      PB[it % 2][:, :]), "s_vch", 1)
                    wch(v)
                    inc(v.tensor_copy(xrf32[:, tt * 128:(tt + 1) * 128],
                                      PB[it % 2][:, :]), "s_evt", 1)
                inc(v.tensor_copy(xrf32[:, 0:1], xrf32[:, 0:1]), "s_evt2", 1)
            w(v, "s_mmr", 16)
            inc(v.tensor_copy(logits_fm[:, :], PL[:, :]), "s_lfm", 1)
            for tt in range(8):
                w(v, "s_gpe", 2 * tt + 1)
                inc(v.tensor_copy(ltm[:, :], pgt), "s_gltm", 1)
                w(v, "s_gltm", tt + 1)
                inc(v.reduce_max(gm1[:, :], ltm[:, :], axis=AX.X), "s_vch", 1)
                wch(v)
                inc(v.tensor_scalar(gnm1[:, :], gm1[:, :], -1.0, None, ALU.mult), "s_vch", 1)
                wch(v)
                inc(v.tensor_scalar(gge[:, :], ltm[:, :], gm1[:, :], None, ALU.is_ge), "s_gv2", 1)
                w(v, "s_gv2", tt + 1)
                inc(v.tensor_scalar(gmsk[:, :], gge[:, :], -1e30, None, ALU.mult), "s_vch", 1)
                wch(v)
                inc(v.tensor_add(gmsk[:, :], gmsk[:, :], ltm[:, :]), "s_vch", 1)
                wch(v)
                inc(v.reduce_max(gm2[:, :], gmsk[:, :], axis=AX.X), "s_vch", 1)
                wch(v)
                inc(v.tensor_scalar(gkeep[:, :], ltm[:, :], gm2[:, :], None, ALU.is_ge), "s_vch", 1)
                w(v, "s_gact", tt + 1)
                wch(v)
                inc(v.tensor_mul(gzk[:, :], gz[:, :], gkeep[:, :]), "s_vch", 1)
                wch(v)
                inc(v.reduce_sum(gss[:, :], gzk[:, :], axis=AX.X), "s_vch", 1)
                wch(v)
                inc(v.reciprocal(grs[:, :], gss[:, :]), "s_vch", 1)
                wch(v)
                inc(v.tensor_scalar(gtm[:, :], gzk[:, :], grs[:, :], 2.0,
                                    ALU.mult, ALU.mult), "s_vch", 1)
                wch(v)
                inc(v.tensor_copy(gbf[:, :], gtm[:, :]), "s_gdv", 1)
                w(v, "s_gpe", 2 * tt + 2)
                inc(v.tensor_copy(gate_fm[0:8, tt * 128:(tt + 1) * 128],
                                  PGB[:, :]), "s_gdv", 1)
            for half in range(2):
                w(v, "s_mmg", half + 1)
                inc(v.tensor_copy(grep_sb[:, half * 512:(half + 1) * 512],
                                  PA[half][:, :]), "s_evg", 1)
            for half in range(2):
                w(v, "s_mmm", half + 1)
                inc(v.tensor_mul(midg[:, half * 512:(half + 1) * 512],
                                 PA[half][:, :],
                                 grep_sb[:, half * 512:(half + 1) * 512]), "s_evm", 1)
            for fb in range(4):
                for tt in range(8):
                    gf = fb * 8 + tt
                    w(v, "s_mmf", gf + 1)
                    inc(v.tensor_add(ysb[:, tt, fb * 512:(fb + 1) * 512],
                                     ysb[:, tt, fb * 512:(fb + 1) * 512],
                                     PA[gf % 2][:, :]), "s_yf", 1)
            for tt in range(8):
                emit_ln_vector(v, tt, "s_ln2a", "s_ln2v", "s_yf", 24 + tt + 1)

    phB.close()
    st.close()
    return nc, dbg_t


_CACHE = {}


def _get_nc(dbg=()):
    key = tuple(sorted(dbg))
    if key not in _CACHE:
        _CACHE[key] = build_nc(dbg)
    return _CACHE[key]


def make_in_maps(x, w_qkv, w_o, base_w, base_b, router_w, router_b, lora_A, lora_B):
    ac = np.ascontiguousarray
    in_maps = []
    for c in range(8):
        b = c // 4
        hq = c % 4
        xb = x[b]
        wq = w_qkv[:, hq * 512:(hq + 1) * 512]
        wk = w_qkv[:, 2048 + hq * 512:2048 + (hq + 1) * 512]
        in_maps.append({
            "x_full": ac(xb), "x_own": ac(xb[hq * TOK:(hq + 1) * TOK]),
            "wqk": ac(np.concatenate([wq, wk], axis=1)),
            "wv": ac(w_qkv[:, 4096:]), "wo": ac(w_o), "bw": ac(base_w),
            "bb_in": ac(base_b.reshape(1, D)),
            "rw_in": ac(router_w), "rb_in": ac(router_b.reshape(1, 8)),
            "lA_in": ac(lora_A), "lB_in": ac(lora_B.reshape(128, D)),
            "hq_in": np.array([[hq]], dtype=np.int32),
        })
    return in_maps


def kernel(L0_attn_x, L0_attn_w_qkv, L0_attn_w_o, L0_ffn_base_w, L0_ffn_base_b,
           L0_ffn_router_w, L0_ffn_router_b, L0_ffn_lora_A, L0_ffn_lora_B):
    x = np.asarray(L0_attn_x, dtype=np.float32)
    in_maps = make_in_maps(x, np.asarray(L0_attn_w_qkv), np.asarray(L0_attn_w_o),
                           np.asarray(L0_ffn_base_w), np.asarray(L0_ffn_base_b),
                           np.asarray(L0_ffn_router_w), np.asarray(L0_ffn_router_b),
                           np.asarray(L0_ffn_lora_A), np.asarray(L0_ffn_lora_B))
    nc, _ = _get_nc()
    res = run_bass_kernel_spmd(nc, in_maps, core_ids=list(range(8)))
    out = np.zeros((2, S, D), dtype=np.float32)
    for c in range(8):
        b = c // 4
        hq = c % 4
        out[b, hq * TOK:(hq + 1) * TOK, :] = res.results[c]["y_out"]
    return out


# version 07 (final)
# speedup vs baseline: 1.1178x; 1.1178x over previous
import math
import numpy as np
from contextlib import ExitStack

import concourse.bass as bass
import concourse.mybir as mybir
from concourse.bass_utils import run_bass_kernel_spmd

F32 = mybir.dt.float32
BF16 = mybir.dt.bfloat16
I32 = mybir.dt.int32
AF = mybir.ActivationFunctionType
ALU = mybir.AluOpType
AX = mybir.AxisListType

S = 4096
D = 2048
TOK = 1024
NQC = 32
SCALE = 1.0 / math.sqrt(128.0)
EBIAS = -12.0
EPS = 1e-5


def build_nc(dbg=()):
    nc = bass.Bass("TRN2", target_bir_lowering=False)
    st = ExitStack()
    sems = {}
    cnt = {}

    def sem(name):
        if name not in sems:
            sems[name] = st.enter_context(nc.semaphore(name))
            cnt[name] = 0
        return sems[name]

    def inc(inst, name, n=1):
        inst.then_inc(sem(name), n)
        cnt[name] += n
        return inst

    def w(eng, name, val):
        eng.wait_ge(sem(name), val)

    def wch(eng):
        eng.wait_ge(sem("s_vch"), cnt["s_vch"])

    x_full = nc.dram_tensor("x_full", [S, D], F32, kind="ExternalInput").ap()
    x_own = nc.dram_tensor("x_own", [TOK, D], F32, kind="ExternalInput").ap()
    wqk = nc.dram_tensor("wqk", [D, 1024], F32, kind="ExternalInput").ap()
    wv = nc.dram_tensor("wv", [D, D], F32, kind="ExternalInput").ap()
    wo = nc.dram_tensor("wo", [D, D], F32, kind="ExternalInput").ap()
    bw = nc.dram_tensor("bw", [D, D], F32, kind="ExternalInput").ap()
    bb_in = nc.dram_tensor("bb_in", [1, D], F32, kind="ExternalInput").ap()
    rw_in = nc.dram_tensor("rw_in", [8, D], F32, kind="ExternalInput").ap()
    rb_in = nc.dram_tensor("rb_in", [1, 8], F32, kind="ExternalInput").ap()
    lA_in = nc.dram_tensor("lA_in", [8, D, 16], F32, kind="ExternalInput").ap()
    lB_in = nc.dram_tensor("lB_in", [128, D], F32, kind="ExternalInput").ap()
    hq_in = nc.dram_tensor("hq_in", [1, 1], I32, kind="ExternalInput").ap()

    y_out = nc.dram_tensor("y_out", [TOK, D], F32, kind="ExternalOutput").ap()

    x_bf = nc.dram_tensor("x_bf", [S, D], BF16, kind="Internal").ap()
    xo_bf = nc.dram_tensor("xo_bf", [TOK, D], BF16, kind="Internal").ap()
    bw_bf = nc.dram_tensor("bw_bf", [D, D], BF16, kind="Internal").ap()
    cin = nc.dram_tensor("cin", [4, S], F32, kind="Internal")
    cout = nc.dram_tensor("cout", [16, S], F32, kind="Internal")

    dbg_t = {}
    def dbgo(name, shape, dt=F32):
        if name in dbg:
            dbg_t[name] = nc.dram_tensor("dbg_" + name, shape, dt, kind="ExternalOutput").ap()
    dbgo("cout", [16, S])

    sb = st.enter_context
    ident_bf = sb(nc.sbuf_tensor("ident_bf", [128, 128], BF16))
    ident_f32 = sb(nc.sbuf_tensor("ident_f32", [128, 128], F32))
    ones_row = sb(nc.sbuf_tensor("ones_row", [1, 128], F32))
    ones_tok = sb(nc.sbuf_tensor("ones_tok", [1, 512], F32))
    ones_col_bf = sb(nc.sbuf_tensor("ones_col_bf", [128, 1], BF16))
    e8_bf = sb(nc.sbuf_tensor("e8_bf", [8, 128], BF16))
    bias_exp = sb(nc.sbuf_tensor("bias_exp", [128, 1], F32))
    rwT = sb(nc.sbuf_tensor("rwT", [128, 16, 8], F32))
    hqsb = sb(nc.sbuf_tensor("hqsb", [1, 1], I32))
    vT = sb(nc.sbuf_tensor("vT", [128, 16, TOK], BF16))

    rhq = st.enter_context(nc.gpsimd.register("rhq"))
    ro1 = st.enter_context(nc.gpsimd.register("ro1"))

    phA = ExitStack()
    sbA = phA.enter_context
    qk_sb = sbA(nc.sbuf_tensor("qk_sb", [128, 4, 2, S], BF16))
    xT_sb = sbA(nc.sbuf_tensor("xT_sb", [128, 16, 1024], BF16))
    wqk_sb = sbA(nc.sbuf_tensor("wqk_sb", [128, 2, 16, 128], BF16))
    ekeep = [sbA(nc.sbuf_tensor(f"ekeep{i}", [128, S], BF16)) for i in range(2)]
    Cp = [sbA(nc.sbuf_tensor(f"Cp{i}", [128, S], BF16)) for i in range(2)]
    dpart = [sbA(nc.sbuf_tensor(f"dpart{i}", [128, 4], F32)) for i in range(2)]
    dsum = sbA(nc.sbuf_tensor("dsum", [128, 1], F32))
    rrec = sbA(nc.sbuf_tensor("rrec", [128, 1], F32))
    Ptmp = sbA(nc.sbuf_tensor("Ptmp", [128, S], BF16))
    wvsb = sbA(nc.sbuf_tensor("wvsb", [128, 16, 128], BF16))
    w_stage = [sbA(nc.sbuf_tensor(f"wst{i}", [1, 512], F32)) for i in range(2)]
    w2st = [sbA(nc.sbuf_tensor(f"w2st{i}", [1, TOK], F32)) for i in range(2)]

    def g1a(qt, hl, f, sl):
        return ((qt * 4 + hl) * 2 + f) * 2 + sl

    psA_ctx = nc.psum_tensor("sA", [128, 1024], F32)
    psB_ctx = nc.psum_tensor("sB", [128, 1024], F32)
    psw0_ctx = nc.psum_tensor("psw0", [1, 512], F32)
    psw1_ctx = nc.psum_tensor("psw1", [1, 512], F32)
    pq0_ctx = nc.psum_tensor("pq0", [128, 512], F32)
    pq1_ctx = nc.psum_tensor("pq1", [128, 512], F32)
    sA = phA.enter_context(psA_ctx)
    sB = phA.enter_context(psB_ctx)
    psw = [phA.enter_context(psw0_ctx), phA.enter_context(psw1_ctx)]
    pq = [phA.enter_context(pq0_ctx), phA.enter_context(pq1_ctx)]
    sT = [sA, sB]
    pw = [sA, sB]

    def emit_proj_group(t, gi, qt, hl, f, sl):
        k2 = (qt * 4 + hl) * 2 + f
        if sl == 0:
            if hl == 0 and f == 0:
                w(t, "s_xt", 256 * (qt + 1))
            w(t, "s_wk", 16 * (k2 + 1))
        if gi >= 2:
            w(t, "s_evq", gi - 1)
        for dc in range(16):
            mi = t.matmul(pq[gi % 2][:, :], wqk_sb[:, k2 % 2, dc, :],
                          xT_sb[:, dc, sl * 512:(sl + 1) * 512],
                          start=(dc == 0), stop=(dc == 15))
        inc(mi, "s_mmq", 1)

    def emit_v_group(t, gv):
        fc, sl = divmod(gv, 2)
        if gv == 0:
            w(t, "s_xtv", 256)
            w(t, "s_evq", 64)
        w(t, "s_wv", 16 * (fc + 1))
        if gv >= 2:
            w(t, "s_evv", gv - 1)
        for dc in range(16):
            mi = t.matmul(pq[gv % 2][:, :], wvsb[:, dc, :],
                          xT_sb[:, dc, sl * 512:sl * 512 + 512],
                          start=(dc == 0), stop=(dc == 15))
        inc(mi, "s_mmv", 1)

    def emit_scores_chunk(t, hs, j):
        J = hs * NQC + j
        for i in range(4):
            Pi = J * 4 + i
            if Pi >= 2:
                w(t, "s_exp", Pi - 1)
            ps = sT[Pi % 2]
            lhs = qk_sb[:, hs, 0, j * 128:(j + 1) * 128]
            for half in range(2):
                k0 = i * 1024 + half * 512
                mi = t.matmul(ps[:, half * 512:(half + 1) * 512],
                              lhs, qk_sb[:, hs, 1, k0:k0 + 512],
                              start=True, stop=True)
            inc(mi, "s_sc", 1)

    def emit_fold(t, hs):
        w(t, "s_mac", NQC * (hs + 1))
        for seg in range(8):
            Sg = hs * 8 + seg
            if Sg >= 2:
                w(t, "s_wev", Sg - 1)
            t.matmul(psw[Sg % 2][:, :], ones_col_bf[:, :],
                     Cp[0][:, seg * 512:(seg + 1) * 512],
                     start=True, stop=False)
            mi = t.matmul(psw[Sg % 2][:, :], ones_col_bf[:, :],
                          Cp[1][:, seg * 512:(seg + 1) * 512],
                          start=False, stop=True)
            inc(mi, "s_fold", 1)

    def emit_mac_chunk(v, hs, j):
        J = hs * NQC + j
        w(v, "s_exp", 4 * (J + 1))
        inc(v.reduce_sum(dsum[:, :], dpart[J % 2][:, 0:4], axis=AX.X), "s_vch", 1)
        wch(v)
        inc(v.reciprocal(rrec[:, :], dsum[:, :]), "s_vch", 1)
        wch(v)
        if j < 2:
            if hs > 0:
                w(v, "s_fold", 8 * hs)
            mac = v.tensor_scalar(Cp[j % 2][:, :], ekeep[J % 2][:, :],
                                  rrec[:, :], None, ALU.mult)
        else:
            inc(v.tensor_scalar(Ptmp[:, :], ekeep[J % 2][:, :],
                                rrec[:, :], None, ALU.mult), "s_vch", 1)
            wch(v)
            mac = v.tensor_tensor(Cp[j % 2][:, :], Cp[j % 2][:, :],
                                  Ptmp[:, :], ALU.add)
        inc(mac, "s_mac", 1)

    def emit_wev(v, hs):
        for seg in range(8):
            Sg = hs * 8 + seg
            w(v, "s_fold", Sg + 1)
            if Sg >= 2:
                w(v, "s_cind", 16 * (Sg - 1))
            inc(v.tensor_copy(w_stage[Sg % 2][:, :], psw[Sg % 2][:, :]),
                "s_wev", 1)

    def emit_evac(v, gi, qt, hl, f, sl):
        w(v, "s_mmq", gi + 1)
        t0 = qt * 1024 + sl * 512
        inc(v.tensor_copy(qk_sb[:, hl, f, t0:t0 + 512], pq[gi % 2][:, :]), "s_evq", 1)

    def emit_evac_v(v, gv):
        fc, sl = divmod(gv, 2)
        w(v, "s_mmv", gv + 1)
        inc(v.tensor_copy(vT[:, fc, sl * 512:(sl + 1) * 512], pq[gv % 2][:, :]),
            "s_evv", 1)

    def weave_v(hs, j, nxtv):
        return hs >= 1 and j % 3 == 0 and nxtv < 32

    with nc.Block("A") as blk:
        @blk.gpsimd
        def _(g):
            g.memset(ident_bf[:, :], 0.0)
            g.affine_select(out=ident_bf[:, :], in_=ident_bf[:, :],
                            compare_op=ALU.not_equal, fill=1.0, base=0,
                            pattern=[[-1, 128]], channel_multiplier=1)
            g.memset(ident_f32[:, :], 0.0)
            g.affine_select(out=ident_f32[:, :], in_=ident_f32[:, :],
                            compare_op=ALU.not_equal, fill=1.0, base=0,
                            pattern=[[-1, 128]], channel_multiplier=1)
            g.memset(ones_row[:, :], 1.0)
            g.memset(ones_tok[:, :], 1.0)
            g.memset(ones_col_bf[:, :], 1.0)
            g.memset(bias_exp[:, :], EBIAS)
            g.memset(e8_bf[:, :], 1.0)
            g.affine_select(out=e8_bf[:, :], in_=e8_bf[:, :], compare_op=ALU.is_ge,
                            fill=0.0, base=0, pattern=[[1, 128]], channel_multiplier=-16)
            inc(g.affine_select(out=e8_bf[:, :], in_=e8_bf[:, :], compare_op=ALU.is_ge,
                                fill=0.0, base=15, pattern=[[-1, 128]], channel_multiplier=16),
                "s_const", 1)
            inc(g.dma_start(hqsb[:, :], hq_in), "s_pn", 16)
            for i in range(4):
                inc(g.dma_start(x_bf[i * 1024:(i + 1) * 1024, :],
                                x_full[i * 1024:(i + 1) * 1024, :]), "s_xc", 16)
            inc(g.dma_start(xo_bf, x_own), "s_xoc", 16)
            for qt in range(4):
                for hl in range(4):
                    for f in range(2):
                        k2 = (qt * 4 + hl) * 2 + f
                        if k2 >= 2:
                            w(g, "s_mmq", 2 * k2 - 2)
                        c0 = f * 512 + hl * 128
                        src = bass.AP(wqk.tensor, c0,
                                      [[1024, 128], [128 * 1024, 16], [1, 128]])
                        inc(g.dma_start(wqk_sb[:, k2 % 2, :, :], src), "s_wk", 16)
            for fc in range(16):
                if fc >= 1:
                    w(g, "s_mmv", 2 * fc)
                src = bass.AP(wv.tensor, fc * 128, [[2048, 128], [128 * 2048, 16], [1, 128]])
                inc(g.dma_start(wvsb[:, :, :], src), "s_wv", 16)
            w(g, "s_cind", 512)
            inc(g.collective_compute(
                "AllGather", ALU.bypass,
                ins=[cin.ap().opt()], outs=[cout.ap().opt()],
                replica_groups=[[0, 1, 2, 3], [4, 5, 6, 7]]), "s_cc", 1)
            w(g, "s_cc", 1)
            w(g, "s_pn", 16)
            g.reg_load(rhq, hqsb[:1, :1])
            g.reg_mul(rhq, rhq, TOK)
            for h in range(16):
                if h >= 2:
                    w(g, "s_pwmm", h - 1)
                g.reg_mov(ro1, rhq)
                g.reg_add(ro1, ro1, h * S)
                inc(g.dma_start(w2st[h % 2][:, :],
                                bass.AP(cout.ap().tensor, ro1, [[S, 1], [1, TOK]])),
                    "s_w16", 16)
            if "cout" in dbg_t:
                inc(g.dma_start(dbg_t["cout"], cout.ap()), "s_dbg", 16)
                w(g, "s_dbg", 16)
            for i in range(2):
                inc(g.dma_start(bw_bf[i * 1024:(i + 1) * 1024, :],
                                bw[i * 1024:(i + 1) * 1024, :]), "s_bwc", 16)

        @blk.sync
        def _(sy):
            for qt in range(4):
                w(sy, "s_xc", 16 * (qt + 1))
                if qt > 0:
                    w(sy, "s_mmq", 16 * qt)
                for dc in range(16):
                    inc(sy.dma_start(xT_sb[:, dc, :],
                                     x_bf[qt * 1024:(qt + 1) * 1024, dc * 128:(dc + 1) * 128],
                                     transpose=True), "s_xt", 16)
            w(sy, "s_xoc", 16)
            w(sy, "s_mmq", 64)
            for dc in range(16):
                inc(sy.dma_start(xT_sb[:, dc, 0:TOK],
                                 xo_bf[:, dc * 128:(dc + 1) * 128],
                                 transpose=True), "s_xtv", 16)
            for Sg in range(32):
                hs, seg = divmod(Sg, 8)
                w(sy, "s_wev", Sg + 1)
                inc(sy.dma_start(cin.ap()[hs:hs + 1, seg * 512:(seg + 1) * 512],
                                 w_stage[Sg % 2][:, :]), "s_cind", 16)

        @blk.tensor
        def _(t):
            w(t, "s_const", 1)
            for qt in range(4):
                for hl in range(4):
                    for f in range(2):
                        for sl in range(2):
                            emit_proj_group(t, g1a(qt, hl, f, sl), qt, hl, f, sl)
            nxtv = 0
            for hs in range(4):
                w(t, "s_evq", g1a(3, hs, 1, 1) + 1)
                for j in range(NQC):
                    emit_scores_chunk(t, hs, j)
                    if weave_v(hs, j, nxtv):
                        emit_v_group(t, nxtv)
                        nxtv += 1
                emit_fold(t, hs)
            while nxtv < 32:
                emit_v_group(t, nxtv)
                nxtv += 1
            w(t, "s_exp", 512)
            for h in range(16):
                w(t, "s_w16", 16 * (h + 1))
                if h >= 2:
                    w(t, "s_evw", h - 1)
                for half in range(2):
                    mi = t.matmul(pw[h % 2][:, half * 512:(half + 1) * 512],
                                  ones_row[:, :],
                                  w2st[h % 2][:, half * 512:(half + 1) * 512],
                                  start=True, stop=True)
                inc(mi, "s_pwmm", 1)

        @blk.scalar
        def _(sc):
            w(sc, "s_const", 1)
            for hs in range(4):
                for j in range(NQC):
                    J = hs * NQC + j
                    for i in range(4):
                        Pi = J * 4 + i
                        w(sc, "s_sc", Pi + 1)
                        if i == 0 and J >= 2:
                            w(sc, "s_mac", J - 1)
                        dst = ekeep[J % 2][:, i * 1024:(i + 1) * 1024]
                        inc(sc.activation(dst, sT[Pi % 2][:, :], AF.Exp,
                                          bias=bias_exp[:, :], scale=SCALE,
                                          accum_out=dpart[J % 2][:, i:i + 1]),
                            "s_exp", 1)

        @blk.vector
        def _(v):
            for qt in range(4):
                for hl in range(4):
                    for f in range(2):
                        for sl in range(2):
                            emit_evac(v, g1a(qt, hl, f, sl), qt, hl, f, sl)
            nxtv = 0
            for hs in range(4):
                for j in range(NQC):
                    emit_mac_chunk(v, hs, j)
                    if weave_v(hs, j, nxtv):
                        emit_evac_v(v, nxtv)
                        nxtv += 1
                emit_wev(v, hs)
            while nxtv < 32:
                emit_evac_v(v, nxtv)
                nxtv += 1
            for k in range(16):
                w(v, "s_pwmm", k + 1)
                w(v, "s_evv", 2 * k + 2)
                inc(v.tensor_mul(vT[:, k, :], vT[:, k, :], pw[k % 2][:, :]),
                    "s_evw", 1)

    phA.close()

    phB = ExitStack()
    sbB = phB.enter_context
    ysb = sbB(nc.sbuf_tensor("ysb", [128, 8, D], F32))
    x1fm = sbB(nc.sbuf_tensor("x1fm", [128, 16, TOK], BF16))
    xrf32 = sbB(nc.sbuf_tensor("xrf32", [128, TOK], F32))
    wslab = sbB(nc.sbuf_tensor("wslab", [128, 16, 512], BF16))
    logits_fm = sbB(nc.sbuf_tensor("logits_fm", [8, TOK], F32))
    gate_fm = sbB(nc.sbuf_tensor("gate_fm", [8, TOK], BF16))
    grep_sb = sbB(nc.sbuf_tensor("grep_sb", [128, TOK], BF16))
    midg = sbB(nc.sbuf_tensor("midg", [128, TOK], BF16))
    sqbuf = sbB(nc.sbuf_tensor("sqbuf", [128, TOK], F32))
    lssqA = sbB(nc.sbuf_tensor("lssqA", [128, 1], F32))
    ltm = sbB(nc.sbuf_tensor("ltm", [128, 8], F32))
    gm1 = sbB(nc.sbuf_tensor("gm1", [128, 1], F32))
    gnm1 = sbB(nc.sbuf_tensor("gnm1", [128, 1], F32))
    gge = sbB(nc.sbuf_tensor("gge", [128, 8], F32))
    gmsk = sbB(nc.sbuf_tensor("gmsk", [128, 8], F32))
    gm2 = sbB(nc.sbuf_tensor("gm2", [128, 1], F32))
    gz = sbB(nc.sbuf_tensor("gz", [128, 8], F32))
    gkeep = sbB(nc.sbuf_tensor("gkeep", [128, 8], F32))
    gzk = sbB(nc.sbuf_tensor("gzk", [128, 8], F32))
    gss = sbB(nc.sbuf_tensor("gss", [128, 1], F32))
    grs = sbB(nc.sbuf_tensor("grs", [128, 1], F32))
    gtm = sbB(nc.sbuf_tensor("gtm", [128, 8], F32))
    gbf = sbB(nc.sbuf_tensor("gbf", [128, 8], BF16))
    ls1 = sbB(nc.sbuf_tensor("ls1", [128, 1], F32))
    lssq = sbB(nc.sbuf_tensor("lssq", [128, 1], F32))
    lmean = sbB(nc.sbuf_tensor("lmean", [128, 1], F32))
    lmsq = sbB(nc.sbuf_tensor("lmsq", [128, 1], F32))
    lm2t = sbB(nc.sbuf_tensor("lm2t", [128, 1], F32))
    lvar = sbB(nc.sbuf_tensor("lvar", [128, 1], F32))
    lsd = sbB(nc.sbuf_tensor("lsd", [128, 1], F32))
    lrstd = sbB(nc.sbuf_tensor("lrstd", [128, 1], F32))
    bb_sb = sbB(nc.sbuf_tensor("bb_sb", [1, D], F32))
    rb_sb = sbB(nc.sbuf_tensor("rb_sb", [1, 8], F32))
    lacat = sbB(nc.sbuf_tensor("lacat", [128, 16, 128], BF16))
    lbcat = sbB(nc.sbuf_tensor("lbcat", [128, D], BF16))
    rw_sb = sbB(nc.sbuf_tensor("rw_sb", [8, 512], F32))

    PA = [phB.enter_context(nc.psum_tensor("PA0", [128, 512], F32)),
          phB.enter_context(nc.psum_tensor("PA1", [128, 512], F32))]
    PB = [phB.enter_context(nc.psum_tensor("PB0", [128, 128], F32)),
          phB.enter_context(nc.psum_tensor("PB1", [128, 128], F32))]
    PL = phB.enter_context(nc.psum_tensor("PL", [8, TOK], F32))
    PGB = phB.enter_context(nc.psum_tensor("PGB", [8, 128], BF16))
    pgt = PB[0][:, 0:8]

    def emit_ln_vector(v, tt, sa_name, sv_name, gate_name, gate_cnt):
        ap = ysb[:, tt, :]
        w(v, gate_name, gate_cnt)
        inc(v.reduce_sum(ls1[:, :], ap, axis=AX.X), "s_vch", 1)
        wch(v)
        inc(v.tensor_scalar(lmean[:, :], ls1[:, :], 1.0 / D, None, ALU.mult), "s_vch", 1)
        w(v, sa_name, 3 * tt + 2)
        inc(v.tensor_add(lssq[:, :], lssq[:, :], lssqA[:, :]), "s_vch", 1)
        wch(v)
        inc(v.tensor_scalar(lmsq[:, :], lssq[:, :], 1.0 / D, None, ALU.mult), "s_vch", 1)
        wch(v)
        inc(v.tensor_mul(lm2t[:, :], lmean[:, :], lmean[:, :]), "s_vch", 1)
        wch(v)
        inc(v.tensor_sub(lvar[:, :], lmsq[:, :], lm2t[:, :]), "s_vch", 1)
        wch(v)
        inc(v.tensor_scalar(lvar[:, :], lvar[:, :], EPS, None, ALU.add), sv_name, 1)
        w(v, sa_name, 3 * tt + 3)
        inc(v.reciprocal(lrstd[:, :], lsd[:, :]), "s_vch", 1)
        wch(v)
        inc(v.tensor_scalar(ap, ap, lmean[:, :], lrstd[:, :],
                            ALU.subtract, ALU.mult), sv_name, 1)

    def emit_ln_scalar(sc, tt, sa_name, sv_name, gate_name, gate_cnt):
        w(sc, gate_name, gate_cnt)
        if tt > 0:
            w(sc, sv_name, 2 * tt - 1)
        inc(sc.activation(sqbuf[:, :], ysb[:, tt, 0:TOK], AF.Square,
                          accum_out=lssqA[:, :]), sa_name, 1)
        inc(sc.activation(sqbuf[:, :], ysb[:, tt, TOK:D], AF.Square,
                          accum_out=lssq[:, :]), sa_name, 1)
        w(sc, sv_name, 2 * tt + 1)
        inc(sc.activation(lsd[:, :], lvar[:, :], AF.Sqrt), sa_name, 1)

    with nc.Block("B") as blk:
        @blk.gpsimd
        def _(g):
            inc(g.dma_start(bb_sb[:, :], bb_in), "s_ldb", 16)
            inc(g.dma_start(rb_sb[:, :], rb_in), "s_ldb", 16)
            inc(g.dma_start(lbcat[:, :], lB_in), "s_ldb", 16)
            inc(g.dma_start(rw_sb[:, :], rw_in[:, 0:512]), "s_rw", 16)
            for dc in range(16):
                src = bass.AP(lA_in.tensor, dc * 128 * 16,
                              [[16, 128], [D * 16, 8], [1, 16]])
                dst = lacat[:, dc, :].rearrange("p (e r) -> p e r", e=8)
                inc(g.dma_start(dst, src), "s_lda", 16)
            for qi in range(1, 4):
                w(g, "s_rwt", 4 * qi)
                inc(g.dma_start(rw_sb[:, :], rw_in[:, qi * 512:(qi + 1) * 512]),
                    "s_rw", 16)
            for fb in range(4):
                if fb >= 1:
                    w(g, "s_mmwo", 8 * fb)
                src = bass.AP(wo.tensor, fb * 512, [[2048, 128], [128 * 2048, 16], [1, 512]])
                inc(g.dma_start(wslab[:, :, :], src), "s_wo", 16)

        @blk.sync
        def _(sy):
            for tt in range(8):
                inc(sy.dma_start(ysb[:, tt, :], x_own[tt * 128:(tt + 1) * 128, :]),
                    "s_xow", 16)
            w(sy, "s_bwc", 32)
            w(sy, "s_mmwo", 32)
            for fb in range(4):
                if fb >= 1:
                    w(sy, "s_mmf", 8 * fb)
                for dc in range(16):
                    inc(sy.dma_start(wslab[:, dc, :],
                                     bw_bf[fb * 512:(fb + 1) * 512, dc * 128:(dc + 1) * 128],
                                     transpose=True), "s_wbt", 16)
            for tt in range(8):
                w(sy, "s_ln2v", 2 * (tt + 1))
                inc(sy.dma_start(y_out[tt * 128:(tt + 1) * 128, :], ysb[:, tt, :]),
                    "s_od", 16)
            w(sy, "s_od", 128)

        @blk.tensor
        def _(t):
            w(t, "s_rw", 16)
            for dc in range(16):
                if dc >= 1:
                    w(t, "s_rwev", dc)
                if dc % 4 == 0 and dc > 0:
                    w(t, "s_rw", 16 * (dc // 4 + 1))
                inc(t.transpose(PB[0][:, 0:8], rw_sb[:, (dc % 4) * 128:(dc % 4 + 1) * 128],
                                ident_f32[0:8, 0:8]), "s_rwt", 1)
            for fb in range(4):
                w(t, "s_wo", 16 * (fb + 1))
                if fb == 0:
                    w(t, "s_evw", 16)
                for tt in range(8):
                    gw = fb * 8 + tt
                    if gw >= 2:
                        w(t, "s_ya", gw - 1)
                    for cc in range(16):
                        mi = t.matmul(PA[gw % 2][:, :],
                                      vT[:, cc, tt * 128:(tt + 1) * 128],
                                      wslab[:, cc, :],
                                      start=(cc == 0), stop=(cc == 15))
                    inc(mi, "s_mmwo", 1)
            def router_mm(t, dc):
                w(t, "s_evt2", dc + 1)
                for half in range(2):
                    mi = t.matmul(PL[:, half * 512:(half + 1) * 512],
                                  rwT[:, dc, :], xrf32[:, half * 512:(half + 1) * 512],
                                  start=False, stop=(dc == 15))
                inc(mi, "s_mmr", 1)
            w(t, "s_ldb", 32)
            t.matmul(PL[:, 0:512], rb_sb[:, :], ones_tok[:, :], start=True, stop=False)
            t.matmul(PL[:, 512:1024], rb_sb[:, :], ones_tok[:, :], start=True, stop=False)
            for dc in range(16):
                if dc >= 1:
                    router_mm(t, dc - 1)
                for tt in range(8):
                    it = dc * 8 + tt
                    if dc == 0:
                        w(t, "s_ln1v", 2 * tt + 2)
                    if it >= 2:
                        w(t, "s_evt", it - 1)
                    inc(t.transpose(PB[it % 2][:, :], ysb[:, tt, dc * 128:(dc + 1) * 128],
                                    ident_f32[:, :]), "s_mmt", 1)
            router_mm(t, 15)
            for tt in range(8):
                w(t, "s_lfm", 1)
                if tt > 0:
                    w(t, "s_gltm", tt)
                    w(t, "s_gdv", 2 * tt)
                inc(t.transpose(pgt, logits_fm[0:8, tt * 128:(tt + 1) * 128],
                                ident_f32[0:8, 0:8]), "s_gpe", 1)
                w(t, "s_gdv", 2 * tt + 1)
                inc(t.transpose(PGB[:, :], gbf[:, :], ident_bf[:, :]), "s_gpe", 1)
            w(t, "s_gdv", 16)
            w(t, "s_lda", 256)
            for half in range(2):
                inc(t.matmul(PA[half][:, :], e8_bf[:, :],
                             gate_fm[:, half * 512:(half + 1) * 512],
                             start=True, stop=True), "s_mmg", 1)
            for half in range(2):
                w(t, "s_evg", half + 1)
                for dc in range(16):
                    mi = t.matmul(PA[half][:, :], lacat[:, dc, :],
                                  x1fm[:, dc, half * 512:(half + 1) * 512],
                                  start=(dc == 0), stop=(dc == 15))
                inc(mi, "s_mmm", 1)
            w(t, "s_evm", 2)
            w(t, "s_ldb", 48)
            for fb in range(4):
                w(t, "s_wbt", 256 * (fb + 1))
                for tt in range(8):
                    gf = fb * 8 + tt
                    if gf >= 2:
                        w(t, "s_yf", gf - 1)
                    t.matmul(PA[gf % 2][:, :], ones_row[:, :],
                             bb_sb[:, fb * 512:(fb + 1) * 512],
                             start=True, stop=False)
                    for dc in range(16):
                        t.matmul(PA[gf % 2][:, :],
                                 x1fm[:, dc, tt * 128:(tt + 1) * 128],
                                 wslab[:, dc, :],
                                 start=False, stop=False)
                    mi = t.matmul(PA[gf % 2][:, :],
                                  midg[:, tt * 128:(tt + 1) * 128],
                                  lbcat[:, fb * 512:(fb + 1) * 512],
                                  start=False, stop=True)
                    inc(mi, "s_mmf", 1)

        @blk.scalar
        def _(sc):
            for tt in range(8):
                emit_ln_scalar(sc, tt, "s_ln1a", "s_ln1v", "s_ya", 24 + tt + 1)
            for tt in range(8):
                w(sc, "s_gv2", tt + 1)
                inc(sc.activation(gz[:, :], ltm[:, :], AF.Exp, bias=gnm1[:, :],
                                  scale=1.0), "s_gact", 1)
            for tt in range(8):
                emit_ln_scalar(sc, tt, "s_ln2a", "s_ln2v", "s_yf", 24 + tt + 1)

        @blk.vector
        def _(v):
            for dc in range(16):
                w(v, "s_rwt", dc + 1)
                inc(v.tensor_copy(rwT[:, dc, :], PB[0][:, 0:8]), "s_rwev", 1)
            for fb in range(4):
                for tt in range(8):
                    gw = fb * 8 + tt
                    w(v, "s_mmwo", gw + 1)
                    if fb == 0:
                        w(v, "s_xow", 16 * (tt + 1))
                    inc(v.tensor_add(ysb[:, tt, fb * 512:(fb + 1) * 512],
                                     ysb[:, tt, fb * 512:(fb + 1) * 512],
                                     PA[gw % 2][:, :]), "s_ya", 1)
            for tt in range(8):
                emit_ln_vector(v, tt, "s_ln1a", "s_ln1v", "s_ya", 24 + tt + 1)
            for dc in range(16):
                if dc > 0:
                    w(v, "s_mmr", dc)
                for tt in range(8):
                    it = dc * 8 + tt
                    w(v, "s_mmt", it + 1)
                    inc(v.tensor_copy(x1fm[:, dc, tt * 128:(tt + 1) * 128],
                                      PB[it % 2][:, :]), "s_vch", 1)
                    wch(v)
                    inc(v.tensor_copy(xrf32[:, tt * 128:(tt + 1) * 128],
                                      PB[it % 2][:, :]), "s_evt", 1)
                inc(v.tensor_copy(xrf32[:, 0:1], xrf32[:, 0:1]), "s_evt2", 1)
            w(v, "s_mmr", 16)
            inc(v.tensor_copy(logits_fm[:, :], PL[:, :]), "s_lfm", 1)
            for tt in range(8):
                w(v, "s_gpe", 2 * tt + 1)
                inc(v.tensor_copy(ltm[:, :], pgt), "s_gltm", 1)
                w(v, "s_gltm", tt + 1)
                inc(v.reduce_max(gm1[:, :], ltm[:, :], axis=AX.X), "s_vch", 1)
                wch(v)
                inc(v.tensor_scalar(gnm1[:, :], gm1[:, :], -1.0, None, ALU.mult), "s_vch", 1)
                wch(v)
                inc(v.tensor_scalar(gge[:, :], ltm[:, :], gm1[:, :], None, ALU.is_ge), "s_gv2", 1)
                w(v, "s_gv2", tt + 1)
                inc(v.tensor_scalar(gmsk[:, :], gge[:, :], -1e30, None, ALU.mult), "s_vch", 1)
                wch(v)
                inc(v.tensor_add(gmsk[:, :], gmsk[:, :], ltm[:, :]), "s_vch", 1)
                wch(v)
                inc(v.reduce_max(gm2[:, :], gmsk[:, :], axis=AX.X), "s_vch", 1)
                wch(v)
                inc(v.tensor_scalar(gkeep[:, :], ltm[:, :], gm2[:, :], None, ALU.is_ge), "s_vch", 1)
                w(v, "s_gact", tt + 1)
                wch(v)
                inc(v.tensor_mul(gzk[:, :], gz[:, :], gkeep[:, :]), "s_vch", 1)
                wch(v)
                inc(v.reduce_sum(gss[:, :], gzk[:, :], axis=AX.X), "s_vch", 1)
                wch(v)
                inc(v.reciprocal(grs[:, :], gss[:, :]), "s_vch", 1)
                wch(v)
                inc(v.tensor_scalar(gtm[:, :], gzk[:, :], grs[:, :], 2.0,
                                    ALU.mult, ALU.mult), "s_vch", 1)
                wch(v)
                inc(v.tensor_copy(gbf[:, :], gtm[:, :]), "s_gdv", 1)
                w(v, "s_gpe", 2 * tt + 2)
                inc(v.tensor_copy(gate_fm[0:8, tt * 128:(tt + 1) * 128],
                                  PGB[:, :]), "s_gdv", 1)
            for half in range(2):
                w(v, "s_mmg", half + 1)
                inc(v.tensor_copy(grep_sb[:, half * 512:(half + 1) * 512],
                                  PA[half][:, :]), "s_evg", 1)
            for half in range(2):
                w(v, "s_mmm", half + 1)
                inc(v.tensor_mul(midg[:, half * 512:(half + 1) * 512],
                                 PA[half][:, :],
                                 grep_sb[:, half * 512:(half + 1) * 512]), "s_evm", 1)
            for fb in range(4):
                for tt in range(8):
                    gf = fb * 8 + tt
                    w(v, "s_mmf", gf + 1)
                    inc(v.tensor_add(ysb[:, tt, fb * 512:(fb + 1) * 512],
                                     ysb[:, tt, fb * 512:(fb + 1) * 512],
                                     PA[gf % 2][:, :]), "s_yf", 1)
            for tt in range(8):
                emit_ln_vector(v, tt, "s_ln2a", "s_ln2v", "s_yf", 24 + tt + 1)

    phB.close()
    st.close()
    return nc, dbg_t


_CACHE = {}


def _get_nc(dbg=()):
    key = tuple(sorted(dbg))
    if key not in _CACHE:
        _CACHE[key] = build_nc(dbg)
    return _CACHE[key]


def make_in_maps(x, w_qkv, w_o, base_w, base_b, router_w, router_b, lora_A, lora_B):
    ac = np.ascontiguousarray
    in_maps = []
    for c in range(8):
        b = c // 4
        hq = c % 4
        xb = x[b]
        wq = w_qkv[:, hq * 512:(hq + 1) * 512]
        wk = w_qkv[:, 2048 + hq * 512:2048 + (hq + 1) * 512]
        in_maps.append({
            "x_full": ac(xb), "x_own": ac(xb[hq * TOK:(hq + 1) * TOK]),
            "wqk": ac(np.concatenate([wq, wk], axis=1)),
            "wv": ac(w_qkv[:, 4096:]), "wo": ac(w_o), "bw": ac(base_w),
            "bb_in": ac(base_b.reshape(1, D)),
            "rw_in": ac(router_w), "rb_in": ac(router_b.reshape(1, 8)),
            "lA_in": ac(lora_A), "lB_in": ac(lora_B.reshape(128, D)),
            "hq_in": np.array([[hq]], dtype=np.int32),
        })
    return in_maps


def kernel(L0_attn_x, L0_attn_w_qkv, L0_attn_w_o, L0_ffn_base_w, L0_ffn_base_b,
           L0_ffn_router_w, L0_ffn_router_b, L0_ffn_lora_A, L0_ffn_lora_B):
    x = np.asarray(L0_attn_x, dtype=np.float32)
    in_maps = make_in_maps(x, np.asarray(L0_attn_w_qkv), np.asarray(L0_attn_w_o),
                           np.asarray(L0_ffn_base_w), np.asarray(L0_ffn_base_b),
                           np.asarray(L0_ffn_router_w), np.asarray(L0_ffn_router_b),
                           np.asarray(L0_ffn_lora_A), np.asarray(L0_ffn_lora_B))
    nc, _ = _get_nc()
    res = run_bass_kernel_spmd(nc, in_maps, core_ids=list(range(8)))
    out = np.zeros((2, S, D), dtype=np.float32)
    for c in range(8):
        b = c // 4
        hq = c % 4
        out[b, hq * TOK:(hq + 1) * TOK, :] = res.results[c]["y_out"]
    return out


# version 09
# speedup vs baseline: 1.1409x; 1.0207x over previous
import math
import numpy as np
from contextlib import ExitStack

import concourse.bass as bass
import concourse.mybir as mybir
from concourse.bass_utils import run_bass_kernel_spmd

F32 = mybir.dt.float32
BF16 = mybir.dt.bfloat16
I32 = mybir.dt.int32
AF = mybir.ActivationFunctionType
ALU = mybir.AluOpType
AX = mybir.AxisListType

S = 4096
D = 2048
TOK = 1024
NQC = 32
SCALE = 1.0 / math.sqrt(128.0)
EBIAS = -12.0
EPS = 1e-5


def build_nc(dbg=()):
    nc = bass.Bass("TRN2", target_bir_lowering=False)
    st = ExitStack()
    sems = {}
    cnt = {}

    def sem(name):
        if name not in sems:
            sems[name] = st.enter_context(nc.semaphore(name))
            cnt[name] = 0
        return sems[name]

    def inc(inst, name, n=1):
        inst.then_inc(sem(name), n)
        cnt[name] += n
        return inst

    def w(eng, name, val):
        eng.wait_ge(sem(name), val)

    def wch(eng):
        eng.wait_ge(sem("s_vch"), cnt["s_vch"])

    x_full = nc.dram_tensor("x_full", [S, D], F32, kind="ExternalInput").ap()
    x_own = nc.dram_tensor("x_own", [TOK, D], F32, kind="ExternalInput").ap()
    wqk = nc.dram_tensor("wqk", [D, 1024], F32, kind="ExternalInput").ap()
    wv = nc.dram_tensor("wv", [D, D], F32, kind="ExternalInput").ap()
    wo = nc.dram_tensor("wo", [D, D], F32, kind="ExternalInput").ap()
    bw = nc.dram_tensor("bw", [D, D], F32, kind="ExternalInput").ap()
    bb_in = nc.dram_tensor("bb_in", [1, D], F32, kind="ExternalInput").ap()
    rw_in = nc.dram_tensor("rw_in", [8, D], F32, kind="ExternalInput").ap()
    rb_in = nc.dram_tensor("rb_in", [1, 8], F32, kind="ExternalInput").ap()
    lA_in = nc.dram_tensor("lA_in", [8, D, 16], F32, kind="ExternalInput").ap()
    lB_in = nc.dram_tensor("lB_in", [128, D], F32, kind="ExternalInput").ap()
    hq_in = nc.dram_tensor("hq_in", [1, 1], I32, kind="ExternalInput").ap()

    y_out = nc.dram_tensor("y_out", [TOK, D], F32, kind="ExternalOutput").ap()

    x_bf = nc.dram_tensor("x_bf", [S, D], BF16, kind="Internal").ap()
    xo_bf = nc.dram_tensor("xo_bf", [TOK, D], BF16, kind="Internal").ap()
    bw_bf = nc.dram_tensor("bw_bf", [D, D], BF16, kind="Internal").ap()
    cin = nc.dram_tensor("cin", [4, S], F32, kind="Internal")
    cout = nc.dram_tensor("cout", [16, S], F32, kind="Internal")

    dbg_t = {}
    def dbgo(name, shape, dt=F32):
        if name in dbg:
            dbg_t[name] = nc.dram_tensor("dbg_" + name, shape, dt, kind="ExternalOutput").ap()
    dbgo("cout", [16, S])

    sb = st.enter_context
    ident_bf = sb(nc.sbuf_tensor("ident_bf", [128, 128], BF16))
    ident_f32 = sb(nc.sbuf_tensor("ident_f32", [128, 128], F32))
    ones_row = sb(nc.sbuf_tensor("ones_row", [1, 128], F32))
    ones_tok = sb(nc.sbuf_tensor("ones_tok", [1, 512], F32))
    ones_col_bf = sb(nc.sbuf_tensor("ones_col_bf", [128, 1], BF16))
    e8_bf = sb(nc.sbuf_tensor("e8_bf", [8, 128], BF16))
    bias_exp = sb(nc.sbuf_tensor("bias_exp", [128, 1], F32))
    rwT = sb(nc.sbuf_tensor("rwT", [128, 16, 8], F32))
    hqsb = sb(nc.sbuf_tensor("hqsb", [1, 1], I32))
    vT = sb(nc.sbuf_tensor("vT", [128, 16, TOK], BF16))

    rhq = st.enter_context(nc.gpsimd.register("rhq"))
    ro1 = st.enter_context(nc.gpsimd.register("ro1"))

    phA = ExitStack()
    sbA = phA.enter_context
    qk_sb = sbA(nc.sbuf_tensor("qk_sb", [128, 4, 2, S], BF16))
    xT_sb = sbA(nc.sbuf_tensor("xT_sb", [128, 16, 1024], BF16))
    wqk_sb = sbA(nc.sbuf_tensor("wqk_sb", [128, 2, 16, 128], BF16))
    ekeep = [sbA(nc.sbuf_tensor(f"ekeep{i}", [128, S], BF16)) for i in range(2)]
    Cp = [sbA(nc.sbuf_tensor(f"Cp{i}", [128, S], BF16)) for i in range(2)]
    dpart = [sbA(nc.sbuf_tensor(f"dpart{i}", [128, 4], F32)) for i in range(2)]
    dsum = sbA(nc.sbuf_tensor("dsum", [128, 1], F32))
    rrec = sbA(nc.sbuf_tensor("rrec", [128, 1], F32))
    Ptmp = sbA(nc.sbuf_tensor("Ptmp", [128, S], BF16))
    wvsb = sbA(nc.sbuf_tensor("wvsb", [128, 16, 128], BF16))
    w_stage = [sbA(nc.sbuf_tensor(f"wst{i}", [1, 512], F32)) for i in range(2)]
    w2st = [sbA(nc.sbuf_tensor(f"w2st{i}", [1, TOK], F32)) for i in range(2)]

    def g1a(qt, hl, f, sl):
        return ((qt * 4 + hl) * 2 + f) * 2 + sl

    psA_ctx = nc.psum_tensor("sA", [128, 1024], F32)
    psB_ctx = nc.psum_tensor("sB", [128, 1024], F32)
    psw0_ctx = nc.psum_tensor("psw0", [1, 512], F32)
    psw1_ctx = nc.psum_tensor("psw1", [1, 512], F32)
    pq0_ctx = nc.psum_tensor("pq0", [128, 512], F32)
    pq1_ctx = nc.psum_tensor("pq1", [128, 512], F32)
    sA = phA.enter_context(psA_ctx)
    sB = phA.enter_context(psB_ctx)
    psw = [phA.enter_context(psw0_ctx), phA.enter_context(psw1_ctx)]
    pq = [phA.enter_context(pq0_ctx), phA.enter_context(pq1_ctx)]
    sT = [sA, sB]
    pw = [sA, sB]

    def emit_proj_group(t, gi, qt, hl, f, sl):
        k2 = (qt * 4 + hl) * 2 + f
        if sl == 0:
            if hl == 0 and f == 0 and qt > 0:
                w(t, "s_xt", 256 * (qt + 1))
            w(t, "s_wk", 16 * (k2 + 1))
        if gi >= 2:
            w(t, "s_evq", gi - 1)
        for dc in range(16):
            if gi == 0:
                w(t, "s_xt", 16 * (dc + 1))
            mi = t.matmul(pq[gi % 2][:, :], wqk_sb[:, k2 % 2, dc, :],
                          xT_sb[:, dc, sl * 512:(sl + 1) * 512],
                          start=(dc == 0), stop=(dc == 15))
        inc(mi, "s_mmq", 1)

    def emit_v_group(t, gv):
        fc, sl = divmod(gv, 2)
        if gv == 0:
            w(t, "s_xtv", 256)
            w(t, "s_evq", 64)
        w(t, "s_wv", 16 * (fc + 1))
        if gv >= 2:
            w(t, "s_evv", gv - 1)
        for dc in range(16):
            mi = t.matmul(pq[gv % 2][:, :], wvsb[:, dc, :],
                          xT_sb[:, dc, sl * 512:sl * 512 + 512],
                          start=(dc == 0), stop=(dc == 15))
        inc(mi, "s_mmv", 1)

    def emit_scores_chunk(t, hs, j):
        J = hs * NQC + j
        for i in range(4):
            Pi = J * 4 + i
            if Pi >= 2:
                w(t, "s_exp", Pi - 1)
            ps = sT[Pi % 2]
            lhs = qk_sb[:, hs, 0, j * 128:(j + 1) * 128]
            for half in range(2):
                k0 = i * 1024 + half * 512
                mi = t.matmul(ps[:, half * 512:(half + 1) * 512],
                              lhs, qk_sb[:, hs, 1, k0:k0 + 512],
                              start=True, stop=True)
            inc(mi, "s_sc", 1)

    def emit_fold(t, hs):
        w(t, "s_mac", NQC * (hs + 1))
        for seg in range(8):
            Sg = hs * 8 + seg
            if Sg >= 2:
                w(t, "s_wev", Sg - 1)
            t.matmul(psw[Sg % 2][:, :], ones_col_bf[:, :],
                     Cp[0][:, seg * 512:(seg + 1) * 512],
                     start=True, stop=False)
            mi = t.matmul(psw[Sg % 2][:, :], ones_col_bf[:, :],
                          Cp[1][:, seg * 512:(seg + 1) * 512],
                          start=False, stop=True)
            inc(mi, "s_fold", 1)

    def emit_mac_chunk(v, hs, j):
        J = hs * NQC + j
        w(v, "s_exp", 4 * (J + 1))
        inc(v.reduce_sum(dsum[:, :], dpart[J % 2][:, 0:4], axis=AX.X), "s_vch", 1)
        wch(v)
        inc(v.reciprocal(rrec[:, :], dsum[:, :]), "s_vch", 1)
        wch(v)
        if j < 2:
            if hs > 0:
                w(v, "s_fold", 8 * hs)
            mac = v.tensor_scalar(Cp[j % 2][:, :], ekeep[J % 2][:, :],
                                  rrec[:, :], None, ALU.mult)
        else:
            inc(v.tensor_scalar(Ptmp[:, :], ekeep[J % 2][:, :],
                                rrec[:, :], None, ALU.mult), "s_vch", 1)
            wch(v)
            inc(v.tensor_tensor(Cp[j % 2][:, 0:2048], Cp[j % 2][:, 0:2048],
                                Ptmp[:, 0:2048], ALU.add), "s_vch", 1)
            mac = v.tensor_tensor(Cp[j % 2][:, 2048:S], Cp[j % 2][:, 2048:S],
                                  Ptmp[:, 2048:S], ALU.add)
        inc(mac, "s_mac", 1)

    def emit_wev(v, hs):
        for seg in range(8):
            Sg = hs * 8 + seg
            w(v, "s_fold", Sg + 1)
            if Sg >= 2:
                w(v, "s_cind", 16 * (Sg - 1))
            inc(v.tensor_copy(w_stage[Sg % 2][:, :], psw[Sg % 2][:, :]),
                "s_wev", 1)

    def emit_evac(v, gi, qt, hl, f, sl):
        w(v, "s_mmq", gi + 1)
        t0 = qt * 1024 + sl * 512
        inc(v.tensor_copy(qk_sb[:, hl, f, t0:t0 + 512], pq[gi % 2][:, :]), "s_evq", 1)

    def emit_evac_v(v, gv):
        fc, sl = divmod(gv, 2)
        w(v, "s_mmv", gv + 1)
        inc(v.tensor_copy(vT[:, fc, sl * 512:(sl + 1) * 512], pq[gv % 2][:, :]),
            "s_evv", 1)

    def weave_v(hs, j, nxtv):
        return hs >= 1 and j % 3 == 0 and nxtv < 32

    with nc.Block("A") as blk:
        @blk.gpsimd
        def _(g):
            g.memset(ident_bf[:, :], 0.0)
            g.affine_select(out=ident_bf[:, :], in_=ident_bf[:, :],
                            compare_op=ALU.not_equal, fill=1.0, base=0,
                            pattern=[[-1, 128]], channel_multiplier=1)
            g.memset(ident_f32[:, :], 0.0)
            g.affine_select(out=ident_f32[:, :], in_=ident_f32[:, :],
                            compare_op=ALU.not_equal, fill=1.0, base=0,
                            pattern=[[-1, 128]], channel_multiplier=1)
            g.memset(ones_row[:, :], 1.0)
            g.memset(ones_tok[:, :], 1.0)
            g.memset(ones_col_bf[:, :], 1.0)
            g.memset(bias_exp[:, :], EBIAS)
            g.memset(e8_bf[:, :], 1.0)
            g.affine_select(out=e8_bf[:, :], in_=e8_bf[:, :], compare_op=ALU.is_ge,
                            fill=0.0, base=0, pattern=[[1, 128]], channel_multiplier=-16)
            inc(g.affine_select(out=e8_bf[:, :], in_=e8_bf[:, :], compare_op=ALU.is_ge,
                                fill=0.0, base=15, pattern=[[-1, 128]], channel_multiplier=16),
                "s_const", 1)
            inc(g.dma_start(hqsb[:, :], hq_in), "s_pn", 16)

            def wqk_load(k2):
                if k2 >= 2:
                    w(g, "s_mmq", 2 * k2 - 2)
                hl, f = divmod(k2 % 8, 2)
                c0 = f * 512 + hl * 128
                src = bass.AP(wqk.tensor, c0,
                              [[1024, 128], [128 * 1024, 16], [1, 128]])
                inc(g.dma_start(wqk_sb[:, k2 % 2, :, :], src), "s_wk", 16)

            wqk_load(0)
            wqk_load(1)
            inc(g.dma_start(x_bf[0:1024, :], x_full[0:1024, :]), "s_xc", 16)
            inc(g.dma_start(x_bf[1024:2048, :], x_full[1024:2048, :]), "s_xc", 16)
            wqk_load(2)
            wqk_load(3)
            inc(g.dma_start(x_bf[2048:3072, :], x_full[2048:3072, :]), "s_xc", 16)
            inc(g.dma_start(x_bf[3072:4096, :], x_full[3072:4096, :]), "s_xc", 16)
            inc(g.dma_start(xo_bf, x_own), "s_xoc", 16)
            for k2 in range(4, 32):
                wqk_load(k2)
            for i in range(2):
                inc(g.dma_start(bw_bf[i * 1024:(i + 1) * 1024, :],
                                bw[i * 1024:(i + 1) * 1024, :]), "s_bwc", 16)
            for fc in range(16):
                if fc >= 1:
                    w(g, "s_mmv", 2 * fc)
                src = bass.AP(wv.tensor, fc * 128, [[2048, 128], [128 * 2048, 16], [1, 128]])
                inc(g.dma_start(wvsb[:, :, :], src), "s_wv", 16)
            w(g, "s_cind", 512)
            inc(g.collective_compute(
                "AllGather", ALU.bypass,
                ins=[cin.ap().opt()], outs=[cout.ap().opt()],
                replica_groups=[[0, 1, 2, 3], [4, 5, 6, 7]]), "s_cc", 1)
            w(g, "s_cc", 1)
            w(g, "s_pn", 16)
            g.reg_load(rhq, hqsb[:1, :1])
            g.reg_mul(rhq, rhq, TOK)
            for h in range(16):
                if h >= 2:
                    w(g, "s_pwmm", h - 1)
                g.reg_mov(ro1, rhq)
                g.reg_add(ro1, ro1, h * S)
                inc(g.dma_start(w2st[h % 2][:, :],
                                bass.AP(cout.ap().tensor, ro1, [[S, 1], [1, TOK]])),
                    "s_w16", 16)
            if "cout" in dbg_t:
                inc(g.dma_start(dbg_t["cout"], cout.ap()), "s_dbg", 16)
                w(g, "s_dbg", 16)

        @blk.sync
        def _(sy):
            for qt in range(4):
                w(sy, "s_xc", 16 * (qt + 1))
                if qt > 0:
                    w(sy, "s_mmq", 16 * qt)
                for dc in range(16):
                    inc(sy.dma_start(xT_sb[:, dc, :],
                                     x_bf[qt * 1024:(qt + 1) * 1024, dc * 128:(dc + 1) * 128],
                                     transpose=True), "s_xt", 16)
            w(sy, "s_xoc", 16)
            w(sy, "s_mmq", 64)
            for dc in range(16):
                inc(sy.dma_start(xT_sb[:, dc, 0:TOK],
                                 xo_bf[:, dc * 128:(dc + 1) * 128],
                                 transpose=True), "s_xtv", 16)
            for Sg in range(32):
                hs, seg = divmod(Sg, 8)
                w(sy, "s_wev", Sg + 1)
                inc(sy.dma_start(cin.ap()[hs:hs + 1, seg * 512:(seg + 1) * 512],
                                 w_stage[Sg % 2][:, :]), "s_cind", 16)

        @blk.tensor
        def _(t):
            w(t, "s_const", 1)
            for qt in range(4):
                for hl in range(4):
                    for f in range(2):
                        for sl in range(2):
                            emit_proj_group(t, g1a(qt, hl, f, sl), qt, hl, f, sl)
            nxtv = 0
            for hs in range(4):
                w(t, "s_evq", g1a(3, hs, 1, 1) + 1)
                for j in range(NQC):
                    emit_scores_chunk(t, hs, j)
                    if weave_v(hs, j, nxtv):
                        emit_v_group(t, nxtv)
                        nxtv += 1
                emit_fold(t, hs)
            while nxtv < 32:
                emit_v_group(t, nxtv)
                nxtv += 1
            w(t, "s_exp", 512)
            for h in range(16):
                w(t, "s_w16", 16 * (h + 1))
                if h >= 2:
                    w(t, "s_evw", h - 1)
                for half in range(2):
                    mi = t.matmul(pw[h % 2][:, half * 512:(half + 1) * 512],
                                  ones_row[:, :],
                                  w2st[h % 2][:, half * 512:(half + 1) * 512],
                                  start=True, stop=True)
                inc(mi, "s_pwmm", 1)

        @blk.scalar
        def _(sc):
            w(sc, "s_const", 1)
            for hs in range(4):
                for j in range(NQC):
                    J = hs * NQC + j
                    for i in range(4):
                        Pi = J * 4 + i
                        w(sc, "s_sc", Pi + 1)
                        if i == 0 and J >= 2:
                            w(sc, "s_mac", J - 1)
                        dst = ekeep[J % 2][:, i * 1024:(i + 1) * 1024]
                        inc(sc.activation(dst, sT[Pi % 2][:, :], AF.Exp,
                                          bias=bias_exp[:, :], scale=SCALE,
                                          accum_out=dpart[J % 2][:, i:i + 1]),
                            "s_exp", 1)

        @blk.vector
        def _(v):
            for qt in range(4):
                for hl in range(4):
                    for f in range(2):
                        for sl in range(2):
                            emit_evac(v, g1a(qt, hl, f, sl), qt, hl, f, sl)
            nxtv = 0
            for hs in range(4):
                for j in range(NQC):
                    emit_mac_chunk(v, hs, j)
                    if weave_v(hs, j, nxtv):
                        emit_evac_v(v, nxtv)
                        nxtv += 1
                emit_wev(v, hs)
            while nxtv < 32:
                emit_evac_v(v, nxtv)
                nxtv += 1
            for k in range(16):
                w(v, "s_pwmm", k + 1)
                w(v, "s_evv", 2 * k + 2)
                inc(v.tensor_mul(vT[:, k, :], vT[:, k, :], pw[k % 2][:, :]),
                    "s_evw", 1)

    phA.close()

    phB = ExitStack()
    sbB = phB.enter_context
    ysb = sbB(nc.sbuf_tensor("ysb", [128, 8, D], F32))
    x1fm = sbB(nc.sbuf_tensor("x1fm", [128, 16, TOK], BF16))
    xrf32 = sbB(nc.sbuf_tensor("xrf32", [128, TOK], F32))
    wslab = sbB(nc.sbuf_tensor("wslab", [128, 16, 512], BF16))
    logits_fm = sbB(nc.sbuf_tensor("logits_fm", [8, TOK], F32))
    gate_fm = sbB(nc.sbuf_tensor("gate_fm", [8, TOK], BF16))
    grep_sb = sbB(nc.sbuf_tensor("grep_sb", [128, TOK], BF16))
    midg = sbB(nc.sbuf_tensor("midg", [128, TOK], BF16))
    sqbuf = sbB(nc.sbuf_tensor("sqbuf", [128, TOK], F32))
    lssqA = sbB(nc.sbuf_tensor("lssqA", [128, 1], F32))
    ltm = sbB(nc.sbuf_tensor("ltm", [128, 8], F32))
    gm1 = sbB(nc.sbuf_tensor("gm1", [128, 1], F32))
    gnm1 = sbB(nc.sbuf_tensor("gnm1", [128, 1], F32))
    gge = sbB(nc.sbuf_tensor("gge", [128, 8], F32))
    gmsk = sbB(nc.sbuf_tensor("gmsk", [128, 8], F32))
    gm2 = sbB(nc.sbuf_tensor("gm2", [128, 1], F32))
    gz = sbB(nc.sbuf_tensor("gz", [128, 8], F32))
    gkeep = sbB(nc.sbuf_tensor("gkeep", [128, 8], F32))
    gzk = sbB(nc.sbuf_tensor("gzk", [128, 8], F32))
    gss = sbB(nc.sbuf_tensor("gss", [128, 1], F32))
    grs = sbB(nc.sbuf_tensor("grs", [128, 1], F32))
    gtm = sbB(nc.sbuf_tensor("gtm", [128, 8], F32))
    gbf = sbB(nc.sbuf_tensor("gbf", [128, 8], BF16))
    ls1 = sbB(nc.sbuf_tensor("ls1", [128, 1], F32))
    lssq = sbB(nc.sbuf_tensor("lssq", [128, 1], F32))
    lmean = sbB(nc.sbuf_tensor("lmean", [128, 1], F32))
    lmsq = sbB(nc.sbuf_tensor("lmsq", [128, 1], F32))
    lm2t = sbB(nc.sbuf_tensor("lm2t", [128, 1], F32))
    lvar = sbB(nc.sbuf_tensor("lvar", [128, 1], F32))
    lsd = sbB(nc.sbuf_tensor("lsd", [128, 1], F32))
    lrstd = sbB(nc.sbuf_tensor("lrstd", [128, 1], F32))
    bb_sb = sbB(nc.sbuf_tensor("bb_sb", [1, D], F32))
    rb_sb = sbB(nc.sbuf_tensor("rb_sb", [1, 8], F32))
    lacat = sbB(nc.sbuf_tensor("lacat", [128, 16, 128], BF16))
    lbcat = sbB(nc.sbuf_tensor("lbcat", [128, D], BF16))
    rw_sb = sbB(nc.sbuf_tensor("rw_sb", [8, 512], F32))

    PA = [phB.enter_context(nc.psum_tensor("PA0", [128, 512], F32)),
          phB.enter_context(nc.psum_tensor("PA1", [128, 512], F32))]
    PB = [phB.enter_context(nc.psum_tensor("PB0", [128, 128], F32)),
          phB.enter_context(nc.psum_tensor("PB1", [128, 128], F32))]
    PL = phB.enter_context(nc.psum_tensor("PL", [8, TOK], F32))
    PGB = phB.enter_context(nc.psum_tensor("PGB", [8, 128], BF16))
    pgt = PB[0][:, 0:8]

    def emit_ln_vector(v, tt, sa_name, sv_name, gate_name, gate_cnt):
        ap = ysb[:, tt, :]
        w(v, gate_name, gate_cnt)
        inc(v.reduce_sum(ls1[:, :], ap, axis=AX.X), "s_vch", 1)
        wch(v)
        inc(v.tensor_scalar(lmean[:, :], ls1[:, :], 1.0 / D, None, ALU.mult), "s_vch", 1)
        w(v, sa_name, 3 * tt + 2)
        inc(v.tensor_add(lssq[:, :], lssq[:, :], lssqA[:, :]), "s_vch", 1)
        wch(v)
        inc(v.tensor_scalar(lmsq[:, :], lssq[:, :], 1.0 / D, None, ALU.mult), "s_vch", 1)
        wch(v)
        inc(v.tensor_mul(lm2t[:, :], lmean[:, :], lmean[:, :]), "s_vch", 1)
        wch(v)
        inc(v.tensor_sub(lvar[:, :], lmsq[:, :], lm2t[:, :]), "s_vch", 1)
        wch(v)
        inc(v.tensor_scalar(lvar[:, :], lvar[:, :], EPS, None, ALU.add), sv_name, 1)
        w(v, sa_name, 3 * tt + 3)
        inc(v.reciprocal(lrstd[:, :], lsd[:, :]), "s_vch", 1)
        wch(v)
        inc(v.tensor_scalar(ap, ap, lmean[:, :], lrstd[:, :],
                            ALU.subtract, ALU.mult), sv_name, 1)

    def emit_ln_scalar(sc, tt, sa_name, sv_name, gate_name, gate_cnt):
        w(sc, gate_name, gate_cnt)
        if tt > 0:
            w(sc, sv_name, 2 * tt - 1)
        inc(sc.activation(sqbuf[:, :], ysb[:, tt, 0:TOK], AF.Square,
                          accum_out=lssqA[:, :]), sa_name, 1)
        inc(sc.activation(sqbuf[:, :], ysb[:, tt, TOK:D], AF.Square,
                          accum_out=lssq[:, :]), sa_name, 1)
        w(sc, sv_name, 2 * tt + 1)
        inc(sc.activation(lsd[:, :], lvar[:, :], AF.Sqrt), sa_name, 1)

    with nc.Block("B") as blk:
        @blk.gpsimd
        def _(g):
            inc(g.dma_start(bb_sb[:, :], bb_in), "s_ldb", 16)
            inc(g.dma_start(rb_sb[:, :], rb_in), "s_ldb", 16)
            inc(g.dma_start(lbcat[:, :], lB_in), "s_ldb", 16)
            inc(g.dma_start(rw_sb[:, :], rw_in[:, 0:512]), "s_rw", 16)
            for dc in range(16):
                src = bass.AP(lA_in.tensor, dc * 128 * 16,
                              [[16, 128], [D * 16, 8], [1, 16]])
                dst = lacat[:, dc, :].rearrange("p (e r) -> p e r", e=8)
                inc(g.dma_start(dst, src), "s_lda", 16)
            for qi in range(1, 4):
                w(g, "s_rwt", 4 * qi)
                inc(g.dma_start(rw_sb[:, :], rw_in[:, qi * 512:(qi + 1) * 512]),
                    "s_rw", 16)
            for fb in range(4):
                if fb >= 1:
                    w(g, "s_mmwo", 8 * fb)
                src = bass.AP(wo.tensor, fb * 512, [[2048, 128], [128 * 2048, 16], [1, 512]])
                inc(g.dma_start(wslab[:, :, :], src), "s_wo", 16)

        @blk.sync
        def _(sy):
            for tt in range(8):
                inc(sy.dma_start(ysb[:, tt, :], x_own[tt * 128:(tt + 1) * 128, :]),
                    "s_xow", 16)
            w(sy, "s_bwc", 32)
            w(sy, "s_mmwo", 32)
            for fb in range(4):
                if fb >= 1:
                    w(sy, "s_mmf", 8 * fb)
                for dc in range(16):
                    inc(sy.dma_start(wslab[:, dc, :],
                                     bw_bf[fb * 512:(fb + 1) * 512, dc * 128:(dc + 1) * 128],
                                     transpose=True), "s_wbt", 16)
            for tt in range(8):
                w(sy, "s_ln2v", 2 * (tt + 1))
                inc(sy.dma_start(y_out[tt * 128:(tt + 1) * 128, :], ysb[:, tt, :]),
                    "s_od", 16)
            w(sy, "s_od", 128)

        @blk.tensor
        def _(t):
            w(t, "s_rw", 16)
            for dc in range(16):
                if dc >= 1:
                    w(t, "s_rwev", dc)
                if dc % 4 == 0 and dc > 0:
                    w(t, "s_rw", 16 * (dc // 4 + 1))
                inc(t.transpose(PB[0][:, 0:8], rw_sb[:, (dc % 4) * 128:(dc % 4 + 1) * 128],
                                ident_f32[0:8, 0:8]), "s_rwt", 1)
            for fb in range(4):
                w(t, "s_wo", 16 * (fb + 1))
                if fb == 0:
                    w(t, "s_evw", 16)
                for tt in range(8):
                    gw = fb * 8 + tt
                    if gw >= 2:
                        w(t, "s_ya", gw - 1)
                    for cc in range(16):
                        mi = t.matmul(PA[gw % 2][:, :],
                                      vT[:, cc, tt * 128:(tt + 1) * 128],
                                      wslab[:, cc, :],
                                      start=(cc == 0), stop=(cc == 15))
                    inc(mi, "s_mmwo", 1)
            def router_mm(t, dc):
                w(t, "s_evt2", dc + 1)
                for half in range(2):
                    mi = t.matmul(PL[:, half * 512:(half + 1) * 512],
                                  rwT[:, dc, :], xrf32[:, half * 512:(half + 1) * 512],
                                  start=False, stop=(dc == 15))
                inc(mi, "s_mmr", 1)
            w(t, "s_ldb", 32)
            t.matmul(PL[:, 0:512], rb_sb[:, :], ones_tok[:, :], start=True, stop=False)
            t.matmul(PL[:, 512:1024], rb_sb[:, :], ones_tok[:, :], start=True, stop=False)
            for dc in range(16):
                if dc >= 1:
                    router_mm(t, dc - 1)
                for tt in range(8):
                    it = dc * 8 + tt
                    if dc == 0:
                        w(t, "s_ln1v", 2 * tt + 2)
                    if it >= 2:
                        w(t, "s_evt", it - 1)
                    inc(t.transpose(PB[it % 2][:, :], ysb[:, tt, dc * 128:(dc + 1) * 128],
                                    ident_f32[:, :]), "s_mmt", 1)
            router_mm(t, 15)
            for tt in range(8):
                w(t, "s_lfm", 1)
                if tt > 0:
                    w(t, "s_gltm", tt)
                    w(t, "s_gdv", 2 * tt)
                inc(t.transpose(pgt, logits_fm[0:8, tt * 128:(tt + 1) * 128],
                                ident_f32[0:8, 0:8]), "s_gpe", 1)
                w(t, "s_gdv", 2 * tt + 1)
                inc(t.transpose(PGB[:, :], gbf[:, :], ident_bf[:, :]), "s_gpe", 1)
            w(t, "s_gdv", 16)
            w(t, "s_lda", 256)
            for half in range(2):
                inc(t.matmul(PA[half][:, :], e8_bf[:, :],
                             gate_fm[:, half * 512:(half + 1) * 512],
                             start=True, stop=True), "s_mmg", 1)
            for half in range(2):
                w(t, "s_evg", half + 1)
                for dc in range(16):
                    mi = t.matmul(PA[half][:, :], lacat[:, dc, :],
                                  x1fm[:, dc, half * 512:(half + 1) * 512],
                                  start=(dc == 0), stop=(dc == 15))
                inc(mi, "s_mmm", 1)
            w(t, "s_evm", 2)
            w(t, "s_ldb", 48)
            for fb in range(4):
                w(t, "s_wbt", 256 * (fb + 1))
                for tt in range(8):
                    gf = fb * 8 + tt
                    if gf >= 2:
                        w(t, "s_yf", gf - 1)
                    t.matmul(PA[gf % 2][:, :], ones_row[:, :],
                             bb_sb[:, fb * 512:(fb + 1) * 512],
                             start=True, stop=False)
                    for dc in range(16):
                        t.matmul(PA[gf % 2][:, :],
                                 x1fm[:, dc, tt * 128:(tt + 1) * 128],
                                 wslab[:, dc, :],
                                 start=False, stop=False)
                    mi = t.matmul(PA[gf % 2][:, :],
                                  midg[:, tt * 128:(tt + 1) * 128],
                                  lbcat[:, fb * 512:(fb + 1) * 512],
                                  start=False, stop=True)
                    inc(mi, "s_mmf", 1)

        @blk.scalar
        def _(sc):
            for tt in range(8):
                emit_ln_scalar(sc, tt, "s_ln1a", "s_ln1v", "s_ya", 24 + tt + 1)
            for tt in range(8):
                w(sc, "s_gv2", tt + 1)
                inc(sc.activation(gz[:, :], ltm[:, :], AF.Exp, bias=gnm1[:, :],
                                  scale=1.0), "s_gact", 1)
            for tt in range(8):
                emit_ln_scalar(sc, tt, "s_ln2a", "s_ln2v", "s_yf", 24 + tt + 1)

        @blk.vector
        def _(v):
            for dc in range(16):
                w(v, "s_rwt", dc + 1)
                inc(v.tensor_copy(rwT[:, dc, :], PB[0][:, 0:8]), "s_rwev", 1)
            for fb in range(4):
                for tt in range(8):
                    gw = fb * 8 + tt
                    w(v, "s_mmwo", gw + 1)
                    if fb == 0:
                        w(v, "s_xow", 16 * (tt + 1))
                    inc(v.tensor_add(ysb[:, tt, fb * 512:(fb + 1) * 512],
                                     ysb[:, tt, fb * 512:(fb + 1) * 512],
                                     PA[gw % 2][:, :]), "s_ya", 1)
            for tt in range(8):
                emit_ln_vector(v, tt, "s_ln1a", "s_ln1v", "s_ya", 24 + tt + 1)
            for dc in range(16):
                if dc > 0:
                    w(v, "s_mmr", dc)
                for tt in range(8):
                    it = dc * 8 + tt
                    w(v, "s_mmt", it + 1)
                    inc(v.tensor_copy(x1fm[:, dc, tt * 128:(tt + 1) * 128],
                                      PB[it % 2][:, :]), "s_vch", 1)
                    wch(v)
                    inc(v.tensor_copy(xrf32[:, tt * 128:(tt + 1) * 128],
                                      PB[it % 2][:, :]), "s_evt", 1)
                inc(v.tensor_copy(xrf32[:, 0:1], xrf32[:, 0:1]), "s_evt2", 1)
            w(v, "s_mmr", 16)
            inc(v.tensor_copy(logits_fm[:, :], PL[:, :]), "s_lfm", 1)
            for tt in range(8):
                w(v, "s_gpe", 2 * tt + 1)
                inc(v.tensor_copy(ltm[:, :], pgt), "s_gltm", 1)
                w(v, "s_gltm", tt + 1)
                inc(v.reduce_max(gm1[:, :], ltm[:, :], axis=AX.X), "s_vch", 1)
                wch(v)
                inc(v.tensor_scalar(gnm1[:, :], gm1[:, :], -1.0, None, ALU.mult), "s_vch", 1)
                wch(v)
                inc(v.tensor_scalar(gge[:, :], ltm[:, :], gm1[:, :], None, ALU.is_ge), "s_gv2", 1)
                w(v, "s_gv2", tt + 1)
                inc(v.tensor_scalar(gmsk[:, :], gge[:, :], -1e30, None, ALU.mult), "s_vch", 1)
                wch(v)
                inc(v.tensor_add(gmsk[:, :], gmsk[:, :], ltm[:, :]), "s_vch", 1)
                wch(v)
                inc(v.reduce_max(gm2[:, :], gmsk[:, :], axis=AX.X), "s_vch", 1)
                wch(v)
                inc(v.tensor_scalar(gkeep[:, :], ltm[:, :], gm2[:, :], None, ALU.is_ge), "s_vch", 1)
                w(v, "s_gact", tt + 1)
                wch(v)
                inc(v.tensor_mul(gzk[:, :], gz[:, :], gkeep[:, :]), "s_vch", 1)
                wch(v)
                inc(v.reduce_sum(gss[:, :], gzk[:, :], axis=AX.X), "s_vch", 1)
                wch(v)
                inc(v.reciprocal(grs[:, :], gss[:, :]), "s_vch", 1)
                wch(v)
                inc(v.tensor_scalar(gtm[:, :], gzk[:, :], grs[:, :], 2.0,
                                    ALU.mult, ALU.mult), "s_vch", 1)
                wch(v)
                inc(v.tensor_copy(gbf[:, :], gtm[:, :]), "s_gdv", 1)
                w(v, "s_gpe", 2 * tt + 2)
                inc(v.tensor_copy(gate_fm[0:8, tt * 128:(tt + 1) * 128],
                                  PGB[:, :]), "s_gdv", 1)
            for half in range(2):
                w(v, "s_mmg", half + 1)
                inc(v.tensor_copy(grep_sb[:, half * 512:(half + 1) * 512],
                                  PA[half][:, :]), "s_evg", 1)
            for half in range(2):
                w(v, "s_mmm", half + 1)
                inc(v.tensor_mul(midg[:, half * 512:(half + 1) * 512],
                                 PA[half][:, :],
                                 grep_sb[:, half * 512:(half + 1) * 512]), "s_evm", 1)
            for fb in range(4):
                for tt in range(8):
                    gf = fb * 8 + tt
                    w(v, "s_mmf", gf + 1)
                    inc(v.tensor_add(ysb[:, tt, fb * 512:(fb + 1) * 512],
                                     ysb[:, tt, fb * 512:(fb + 1) * 512],
                                     PA[gf % 2][:, :]), "s_yf", 1)
            for tt in range(8):
                emit_ln_vector(v, tt, "s_ln2a", "s_ln2v", "s_yf", 24 + tt + 1)

    phB.close()
    st.close()
    return nc, dbg_t


_CACHE = {}


def _get_nc(dbg=()):
    key = tuple(sorted(dbg))
    if key not in _CACHE:
        _CACHE[key] = build_nc(dbg)
    return _CACHE[key]


def make_in_maps(x, w_qkv, w_o, base_w, base_b, router_w, router_b, lora_A, lora_B):
    ac = np.ascontiguousarray
    in_maps = []
    for c in range(8):
        b = c // 4
        hq = c % 4
        xb = x[b]
        wq = w_qkv[:, hq * 512:(hq + 1) * 512]
        wk = w_qkv[:, 2048 + hq * 512:2048 + (hq + 1) * 512]
        in_maps.append({
            "x_full": ac(xb), "x_own": ac(xb[hq * TOK:(hq + 1) * TOK]),
            "wqk": ac(np.concatenate([wq, wk], axis=1)),
            "wv": ac(w_qkv[:, 4096:]), "wo": ac(w_o), "bw": ac(base_w),
            "bb_in": ac(base_b.reshape(1, D)),
            "rw_in": ac(router_w), "rb_in": ac(router_b.reshape(1, 8)),
            "lA_in": ac(lora_A), "lB_in": ac(lora_B.reshape(128, D)),
            "hq_in": np.array([[hq]], dtype=np.int32),
        })
    return in_maps


def kernel(L0_attn_x, L0_attn_w_qkv, L0_attn_w_o, L0_ffn_base_w, L0_ffn_base_b,
           L0_ffn_router_w, L0_ffn_router_b, L0_ffn_lora_A, L0_ffn_lora_B):
    x = np.asarray(L0_attn_x, dtype=np.float32)
    in_maps = make_in_maps(x, np.asarray(L0_attn_w_qkv), np.asarray(L0_attn_w_o),
                           np.asarray(L0_ffn_base_w), np.asarray(L0_ffn_base_b),
                           np.asarray(L0_ffn_router_w), np.asarray(L0_ffn_router_b),
                           np.asarray(L0_ffn_lora_A), np.asarray(L0_ffn_lora_B))
    nc, _ = _get_nc()
    res = run_bass_kernel_spmd(nc, in_maps, core_ids=list(range(8)))
    out = np.zeros((2, S, D), dtype=np.float32)
    for c in range(8):
        b = c // 4
        hq = c % 4
        out[b, hq * TOK:(hq + 1) * TOK, :] = res.results[c]["y_out"]
    return out
